# version 17
# baseline (speedup 1.0000x reference)
import contextlib
import numpy as np

import concourse.bass as bass
import concourse.mybir as mybir
from concourse.bass_utils import run_bass_kernel_spmd

F32 = mybir.dt.float32
BF16 = mybir.dt.bfloat16
AF = mybir.ActivationFunctionType
ALU = mybir.AluOpType
AX = mybir.AxisListType

D = 2048
H = 16
DH = 128
CH = 16
KT = 16
EPS_LN = 1e-5
EPS_IN = 1e-6


class Sem:
    def __init__(self, stack, nc, name):
        self.h = stack.enter_context(nc.semaphore(name))
        self.n = 0

    def inc(self, instr, k=1):
        instr.then_inc(self.h, k)
        self.n += k


def build(nc, T, dbg=False):
    BLK = 512
    NBLK = T // BLK
    NT = BLK // 128

    xin = nc.dram_tensor("x", [T, D], F32, kind="ExternalInput").ap()
    wq = nc.dram_tensor("wq", [128, KT * D], F32, kind="ExternalInput").ap()
    wk = nc.dram_tensor("wk", [128, KT * D], F32, kind="ExternalInput").ap()
    wv = nc.dram_tensor("wv", [128, KT * D], F32, kind="ExternalInput").ap()
    wo = nc.dram_tensor("wo", [128, KT * D], F32, kind="ExternalInput").ap()
    wbg = nc.dram_tensor("wbg", [128, KT * 144], F32, kind="ExternalInput").ap()
    wg2 = nc.dram_tensor("wg2", [128, D], F32, kind="ExternalInput").ap()
    lnw = nc.dram_tensor("lnw", [H, DH], F32, kind="ExternalInput").ap()
    lnb = nc.dram_tensor("lnb", [H, DH], F32, kind="ExternalInput").ap()
    nrmw = nc.dram_tensor("nrmw", [D], F32, kind="ExternalInput").ap()
    wfT = nc.dram_tensor("wfT", [128, KT * D], F32, kind="ExternalInput").ap()
    gwT = nc.dram_tensor("gwT", [128, KT * 8], F32, kind="ExternalInput").ap()
    a_all = nc.dram_tensor("a_all", [128, KT * 128], F32, kind="ExternalInput").ap()
    b_all = nc.dram_tensor("b_all", [128, D], F32, kind="ExternalInput").ap()
    out = nc.dram_tensor("out", [T, D], F32, kind="ExternalOutput").ap()
    wq_b = nc.dram_tensor("wq_b", [128, KT * D], BF16, kind="Internal").ap()
    wk_b = nc.dram_tensor("wk_b", [128, KT * D], BF16, kind="Internal").ap()
    wv_b = nc.dram_tensor("wv_b", [128, KT * D], BF16, kind="Internal").ap()
    wo_b = nc.dram_tensor("wo_b", [128, KT * D], BF16, kind="Internal").ap()
    wfT_b = nc.dram_tensor("wfT_b", [128, KT * D], BF16, kind="Internal").ap()
    wbg_b = nc.dram_tensor("wbg_b", [128, KT * 144], BF16, kind="Internal").ap()
    wg2_b = nc.dram_tensor("wg2_b", [128, D], BF16, kind="Internal").ap()
    a_all_b = nc.dram_tensor("a_all_b", [128, KT * 128], BF16, kind="Internal").ap()
    gwT_b = nc.dram_tensor("gwT_b", [128, KT * 8], BF16, kind="Internal").ap()

    dbg_t = {}
    if dbg:
        for nm, shp in [("d_kqf", [128, H * 2 * BLK]), ("d_u", [128, NT * D]),
                        ("d_qb", [128, NT * D]), ("d_ke", [128, NT * D]),
                        ("d_etan", [128, NT * H]), ("d_sg", [128, NT * D]),
                        ("d_ot", [128, NT * D]), ("d_o2f", [128, KT * BLK]),
                        ("d_attnf", [128, KT * BLK]), ("d_x3f", [128, KT * BLK]),
                        ("d_cwf", [8, BLK]), ("d_zcw", [128, BLK]),
                        ("d_s32", [128, H * DH]), ("d_xh1f", [128, KT * BLK]),
                        ("d_gsb", [128, H * 256]), ("d_vns", [128, H * DH]),
                        ("d_vnpk0", [128, 512]), ("d_kq0", [128, 512]),
                        ("d_vnpk1", [128, 512]), ("d_kq1", [128, 512]),
                        ("d_opk0", [128, 512]), ("d_opk1", [128, 512])]:
            dbg_t[nm] = nc.dram_tensor(nm, shp, F32, kind="ExternalOutput").ap()

    st = contextlib.ExitStack()
    with st:
        sbt = lambda name, shape, dt: st.enter_context(nc.sbuf_tensor(name, shape, dt))
        pst = lambda name, shape, dt: st.enter_context(nc.psum_tensor(name, shape, dt))

        BIGA = sbt("BIGA", [128, KT * BLK], BF16)
        BIGB = sbt("BIGB", [128, H * 2 * BLK], BF16)
        SGB = sbt("SGB", [128, NT * D], BF16)
        KSC = SGB
        UB_U = sbt("UB_U", [128, NT * D], BF16)
        UB_QB = sbt("UB_QB", [128, NT * D], BF16)
        UB_KE = sbt("UB_KE", [128, NT * D], BF16)
        XT = sbt("XT", [128, D], F32)
        ATM = sbt("ATM", [128, D], BF16)
        TMP2 = sbt("TMP2", [128, D], BF16)
        TMP3 = sbt("TMP3", [128, D], BF16)
        WSTR = sbt("WSTR", [128, 2 * D], BF16)
        BALL = sbt("BALL", [128, D], BF16)
        WREP = sbt("WREP", [128, D], BF16)
        WBREP = sbt("WBREP", [128, D], BF16)
        BREP = sbt("BREP", [128, D], BF16)
        NWREP = sbt("NWREP", [128, D], BF16)
        WMIX = sbt("WMIX", [128, 512], F32)
        GMASK = sbt("GMASK", [128, 256], F32)
        IDENT = sbt("IDENT", [128, 128], BF16)
        IDF32 = sbt("IDF32", [8, 8], F32)
        EMAT = sbt("EMAT", [8, 128], BF16)
        S32 = sbt("S32", [128, H * DH], F32)
        SB16 = sbt("SB16", [128, H * DH], BF16)
        VNS = sbt("VNS", [128, H * DH], BF16)
        GSB = sbt("GSB", [128, H * 256], BF16)
        KHAT = sbt("KHAT", [128, 512], F32)
        GWB = sbt("GWB", [128, 512], F32)
        KHB = sbt("KHB", [128, 512], BF16)
        GWBB = sbt("GWBB", [128, 512], BF16)
        WMIXB = sbt("WMIXB", [128, 512], BF16)
        STB = sbt("STB", [128, 12], BF16)
        GSDB = sbt("GSDB", [128, 256], BF16)
        VNPK = sbt("VNPK", [128, 512], BF16)
        OPK = sbt("OPK", [128, 512], BF16)
        UQPK = sbt("UQPK", [128, 1024], BF16)
        STA = sbt("STA", [128, 40], F32)
        EPSA = sbt("EPSA", [128, 2], F32)
        ETAN = sbt("ETAN", [128, NT * H], F32)
        BETAS = sbt("BETAS", [16, BLK], BF16)
        G1F = sbt("G1F", [128, BLK], BF16)
        KS = sbt("KS", [128, NT * H], F32)
        RSK = sbt("RSK", [128, NT * H], F32)
        OGS = sbt("OGS", [128, H], F32)
        MUB = sbt("MUB", [128, NT], F32)
        RB = sbt("RB", [128, NT], F32)
        LGT = sbt("LGT", [8, BLK], F32)
        LTM = sbt("LTM", [128, NT * 8], F32)
        CW1 = sbt("CW1", [128, NT * 8], F32)
        CWB = sbt("CWB", [128, NT * 8], BF16)
        CWF = sbt("CWF", [8, BLK], BF16)
        CWX = sbt("CWX", [128, BLK], BF16)
        ZCW = sbt("ZCW", [128, BLK], BF16)

        KQ = pst("KQ", [128, 2048], F32)
        PP = pst("PP", [128, 1024], F32)
        TP = pst("TP", [128, 1024], BF16)
        GD = pst("GD", [128, 512], F32)

        LD = Sem(st, nc, "LD")
        WL = Sem(st, nc, "WL")
        WH = Sem(st, nc, "WH")
        WP = Sem(st, nc, "WP")
        MM = Sem(st, nc, "MM")
        EV = Sem(st, nc, "EV")
        SE = Sem(st, nc, "SE")
        GP = Sem(st, nc, "GP")
        SH = Sem(st, nc, "SH")
        SO = Sem(st, nc, "SO")
        ALLSEMS = [LD, WL, WH, MM, EV, SE, GP, SH, SO]

        blk = st.enter_context(nc.Block())

        def vfence(v, instr):
            EV.inc(instr, 1)
            v.wait_ge(EV.h, EV.n)

        def barrier():
            snap = [(s, s.n) for s in ALLSEMS]
            for sec in (blk.tensor, blk.vector, blk.scalar, blk.gpsimd, blk.sync):
                def _f(e, snap=snap):
                    for s, n in snap:
                        if n > 0:
                            e.wait_ge(s.h, n)
                sec(_f)

        IOT1 = KHAT
        IOT2 = GWB

        def _setup_g(g):
            g.iota(IOT1[:, 0:256], pattern=[[0, 256]], base=0,
                   channel_multiplier=1, allow_small_or_imprecise_dtypes=True)
            g.iota(IOT2[:, 0:256], pattern=[[16, 8], [0, 32]], base=0,
                   channel_multiplier=0, allow_small_or_imprecise_dtypes=True)
            g.iota(IOT2[:, 256:512], pattern=[[16, 8], [0, 2], [1, 16]], base=0,
                   channel_multiplier=0, allow_small_or_imprecise_dtypes=True)
            g.iota(IOT1[:, 256:384], pattern=[[1, 128]], base=0,
                   channel_multiplier=0, allow_small_or_imprecise_dtypes=True)
            gi = g.iota(KHAT[0:8, 384:512], pattern=[[1, 8], [0, 16]], base=0,
                        channel_multiplier=0, allow_small_or_imprecise_dtypes=True)
            GP.inc(gi, 1)
            for dst, srcr in [(WREP, lnw.rearrange("h d -> (h d)")),
                              (BREP, lnb.rearrange("h d -> (h d)")),
                              (NWREP, nrmw)]:
                dd = g.dma_start(out=dst[:, :], in_=srcr.unsqueeze(0).broadcast_to([128, D]))
                WL.inc(dd, 16)
            for dst, srcw in [(wbg_b, wbg), (wq_b, wq), (wk_b, wk), (wv_b, wv),
                              (wg2_b, wg2), (wo_b, wo), (gwT_b, gwT),
                              (a_all_b, a_all), (wfT_b, wfT)]:
                dd = g.dma_start(out=dst[:, :], in_=srcw[:, :])
                WP.inc(dd, 16)
            dd = g.dma_start(out=BALL[:, :], in_=b_all[:, :])
            WL.inc(dd, 16)
            for g4 in range(4):
                for h4 in range(4):
                    src = lnw[4 * g4 + h4:4 * g4 + h4 + 1, :].broadcast_to([32, DH])
                    dw = g.dma_start(out=WMIX[32 * h4:32 * h4 + 32, 128 * g4:128 * g4 + 128],
                                     in_=src)
                    WL.inc(dw, 16)
        blk.gpsimd(_setup_g)
        gp_iota = GP.n
        wl_setup = WL.n

        def _setup_v(v):
            v.wait_ge(GP.h, gp_iota)
            v.tensor_tensor(IDENT[:, :], IOT1[:, 0:128], IOT1[:, 256:384], ALU.is_equal)
            v.tensor_tensor(IDF32[:, :], IOT1[0:8, 0:8], IOT1[0:8, 256:264], ALU.is_equal)
            v.tensor_tensor(GMASK[:, :].rearrange("p (c x t) -> p c x t", c=8, x=2)[:, :, 0, :],
                            IOT1[:, 0:256].rearrange("p (c x t) -> p c x t", c=8, x=2)[:, :, 0, :],
                            IOT2[:, 0:256].rearrange("p (c x t) -> p c x t", c=8, x=2)[:, :, 0, :],
                            ALU.is_lt)
            v.tensor_tensor(GMASK[:, :].rearrange("p (c x t) -> p c x t", c=8, x=2)[:, :, 1, :],
                            IOT1[:, 0:256].rearrange("p (c x t) -> p c x t", c=8, x=2)[:, :, 1, :],
                            IOT2[:, 256:512].rearrange("p (c x t) -> p c x t", c=8, x=2)[:, :, 1, :],
                            ALU.is_le)
            v.tensor_tensor(EMAT[:, :], IOT1[0:8, 0:128], KHAT[0:8, 384:512], ALU.is_equal)
            v.memset(EPSA[:, 0:1], EPS_LN)
            v.memset(EPSA[:, 1:2], EPS_IN)
            v.memset(S32[:, :], 0.0)
            v.memset(SB16[:, :], 0.0)
            iv = v.memset(VNS[:, :], 0.0)
            v.wait_ge(WL.h, wl_setup)
            for h4 in range(4):
                iv = v.tensor_tensor(WMIX[32 * h4:32 * h4 + 16, :],
                                     WMIX[32 * h4:32 * h4 + 16, :],
                                     WMIX[32 * h4:32 * h4 + 16, :], ALU.mult)
            v.tensor_tensor(WBREP[:, :], WREP[:, :], BREP[:, :], ALU.mult)
            iv = v.tensor_copy(WMIXB[:, :], WMIX[:, :])
            EV.inc(iv, 1)
        blk.vector(_setup_v)
        barrier()

        def stream_wcol(w_ap, n, slot, ncols=128):
            def _g(g):
                src = w_ap[:, n * ncols:(n + 1) * ncols].rearrange("(k p) c -> p k c", p=128)
                dw = g.dma_start(out=WSTR[:, slot * D:slot * D + KT * ncols].rearrange(
                    "p (k c) -> p k c", k=KT), in_=src)
                WL.inc(dw, 16)
            blk.gpsimd(_g)
            return WL.n

        def tp_transposes(srcs, dst_fn, wait=()):
            nsl = len(srcs)

            def _t(t):
                for s, n in wait:
                    t.wait_ge(s.h, n)
                t.wait_ge(EV.h, EV.n)
                for c, sap in enumerate(srcs):
                    it = t.transpose(TP[:, c * 128:(c + 1) * 128], sap, IDENT[:, :])
                MM.inc(it, 1)
            blk.tensor(_t)
            mmn = MM.n

            def _v(v):
                v.wait_ge(MM.h, mmn)
                iv = dst_fn(v, TP[:, 0:nsl * 128])
                EV.inc(iv, 1)
            blk.vector(_v)

        def do_block(b):
            t0 = b * BLK
            nhalf = BLK // 512

            def _pl(s):
                s.preload_act_table(AF.Sqrt)
            blk.scalar(_pl)
            for j in range(NT):
                so_before = SO.n

                def _y(sy, j=j, so_before=so_before):
                    sy.wait_ge(SO.h, so_before)
                    sy.wait_ge(EV.h, EV.n)
                    dx = sy.dma_start(out=XT[:, :], in_=xin[t0 + j * 128:t0 + (j + 1) * 128, :])
                    LD.inc(dx, 16)
                blk.sync(_y)
                ld_x = LD.n

                def _s(s, j=j, ld_x=ld_x):
                    s.wait_ge(LD.h, ld_x)
                    s.wait_ge(EV.h, EV.n)
                    s.activation(PP[:, :], XT[:, 0:1024], AF.Square, accum_out=STA[:, 16:17])
                    i2 = s.activation(KQ[:, 0:1024], XT[:, 1024:2048], AF.Square, accum_out=STA[:, 17:18])
                    SE.inc(i2, 1)
                blk.scalar(_s)
                se_sq = SE.n

                def _v(v, j=j, ld_x=ld_x, se_sq=se_sq):
                    v.wait_ge(LD.h, ld_x)
                    v.tensor_reduce(STA[:, 18:19], XT[:, :], AX.X, ALU.add)
                    v.wait_ge(SE.h, se_sq)
                    v.tensor_add(STA[:, 19:20], STA[:, 16:17], STA[:, 17:18])
                    i0 = v.tensor_tensor(STA[:, 20:21], STA[:, 18:19], STA[:, 18:19], ALU.mult)
                    vfence(v, i0)
                    iv = v.scalar_tensor_tensor(STA[:, 21:22], STA[:, 20:21], -1.0 / D,
                                                STA[:, 19:20], ALU.mult, ALU.add)
                    EV.inc(iv, 1)
                blk.vector(_v)
                ev_vx = EV.n

                def _s2(s, j=j, ev_vx=ev_vx):
                    s.wait_ge(EV.h, ev_vx)
                    i3 = s.activation(STA[:, 22:23], STA[:, 21:22], AF.Sqrt,
                                      bias=EPSA[:, 0:1], scale=1.0 / D)
                    SE.inc(i3, 1)
                blk.scalar(_s2)
                se_std = SE.n

                def _v2(v, j=j, se_std=se_std):
                    v.wait_ge(SE.h, se_std)
                    v.reciprocal(RB[:, j:j + 1], STA[:, 22:23])
                    i0 = v.tensor_scalar_mul(MUB[:, j:j + 1], STA[:, 18:19], 1.0 / D)
                    vfence(v, i0)
                    iv = v.tensor_scalar(ATM[:, :], XT[:, :], MUB[:, j:j + 1],
                                         RB[:, j:j + 1], ALU.subtract, ALU.mult)
                    EV.inc(iv, 1)
                blk.vector(_v2)
                ev_xh = EV.n

                for half in range(2):
                    def mkdst(j=j, half=half):
                        def dst(v, tp):
                            return v.tensor_copy(
                                BIGA[:, :].rearrange("p (k t) -> p k t", k=KT)[
                                    :, half * 8:half * 8 + 8, j * 128:(j + 1) * 128],
                                tp.rearrange("p (c t) -> p c t", c=8))
                        return dst
                    tp_transposes([ATM[:, (half * 8 + c) * 128:(half * 8 + c + 1) * 128]
                                   for c in range(8)], mkdst(), wait=[(EV, ev_xh)])
            barrier()
            if dbg and b == 0:
                def _yd(sy):
                    dd = sy.dma_start(out=dbg_t["d_xh1f"][:, :], in_=BIGA[:, :])
                    WL.inc(dd, 16)
                blk.gpsimd(_yd)

            def _g(g):
                g.wait_ge(WP.h, 16)
                dw = g.dma_start(out=WSTR[:, 0:KT * 144], in_=wbg_b[:, :])
                WH.inc(dw, 16)
            blk.sync(_g)
            wl_wbg = WH.n

            def _t(t, wl_wbg=wl_wbg):
                t.wait_ge(WH.h, wl_wbg)
                for k in range(KT):
                    t.matmul(PP[:, 0:512], WSTR[:, k * 144:k * 144 + 128],
                             BIGA[:, k * BLK:(k + 1) * BLK],
                             start=(k == 0), stop=(k == KT - 1))
                for k in range(KT):
                    it = t.matmul(KQ[0:16, 0:512], WSTR[:, k * 144 + 128:k * 144 + 144],
                                  BIGA[:, k * BLK:(k + 1) * BLK],
                                  start=(k == 0), stop=(k == KT - 1))
                MM.inc(it, 1)
            blk.tensor(_t)
            mm_wbg = MM.n
            big_proj_hist_seed = mm_wbg

            def _s(s, mm_wbg=mm_wbg):
                s.wait_ge(MM.h, mm_wbg)
                s.preload_act_table(AF.Sigmoid)
                s.copy(G1F[:, :], PP[:, 0:BLK])
                i2 = s.activation(BETAS[:, :], KQ[0:16, 0:BLK], AF.Sigmoid)
                SE.inc(i2, 1)
            blk.scalar(_s)
            se_beta = SE.n

            def _t2(t, se_beta=se_beta):
                t.wait_ge(SE.h, se_beta)
                t.wait_ge(EV.h, EV.n)
                for j in range(NT):
                    it = t.transpose(TP[:, j * 16:j * 16 + 16],
                                     BETAS[0:16, j * 128:(j + 1) * 128], IDENT[0:16, 0:16])
                MM.inc(it, 1)
            blk.tensor(_t2)
            mm_eta = MM.n

            def _v(v, mm_eta=mm_eta):
                v.wait_ge(MM.h, mm_eta)
                iv = v.tensor_scalar_mul(ETAN[:, :], TP[:, 0:NT * 16], -2.0)
                EV.inc(iv, 1)
            blk.vector(_v)
            ev_eta = EV.n
            barrier()
            if dbg and b == 0:
                def _yd(sy):
                    dd = sy.dma_start(out=dbg_t["d_etan"][:, :], in_=ETAN[:, :])
                    WL.inc(dd, 16)
                blk.gpsimd(_yd)

            def pslot(s):
                return PP[:, (s % 2) * 512:(s % 2) * 512 + 512] if s < 2 else \
                       KQ[:, (s - 2) * 512:(s - 1) * 512]

            big_proj_state = {"hist": {0: big_proj_hist_seed, 1: big_proj_hist_seed},
                              "se_hist": {2: se_beta, 3: se_beta, 4: se_beta, 5: se_beta},
                              "ev_hist": {}}

            def big_proj(w_ap, src_buf, evac, wp_gate=0):
                for n in range(KT):
                    slot = n % 2
                    mm_prev2 = big_proj_state["hist"].get(slot, 0)

                    def _g(g, n=n, slot=slot, mm_prev2=mm_prev2, wp_gate=wp_gate):
                        if wp_gate:
                            g.wait_ge(WP.h, wp_gate)
                        g.wait_ge(MM.h, mm_prev2)
                        dw = g.dma_start(out=WSTR[:, slot * D:(slot + 1) * D],
                                         in_=w_ap[:, n * D:(n + 1) * D])
                        WH.inc(dw, 16)
                    blk.sync(_g)
                    wl_n = WH.n

                    def _t(t, n=n, slot=slot, wl_n=wl_n,
                           se_prev=big_proj_state["se_hist"].get(n % 6, 0),
                           ev_prev=big_proj_state["ev_hist"].get(n % 6, 0)):
                        t.wait_ge(WH.h, wl_n)
                        if se_prev:
                            t.wait_ge(SE.h, se_prev)
                        if ev_prev:
                            t.wait_ge(EV.h, ev_prev)
                        for k in range(KT):
                            it = t.matmul(pslot(n % 6)[:, k * 0:512] if False else pslot(n % 6),
                                          WSTR[:, slot * D + k * 128:slot * D + (k + 1) * 128],
                                          src_buf[:, k * BLK:(k + 1) * BLK],
                                          start=(k == 0), stop=(k == KT - 1))
                        MM.inc(it, 1)
                    blk.tensor(_t)
                    big_proj_state["hist"][slot] = MM.n
                    evac(n, MM.n, n % 6)
                    big_proj_state["se_hist"][n % 6] = SE.n
                    big_proj_state["ev_hist"][n % 6] = EV.n

            def evac_q(n, mm_n, off=0):
                def _s(s, n=n, mm_n=mm_n, off=off):
                    s.wait_ge(MM.h, mm_n)
                    dstq = BIGB[:, n * 2 * BLK:(n + 1) * 2 * BLK].rearrange(
                        "p (c x t) -> p c x t", x=2, t=16)[:, :, 1, :]
                    s.activation(dstq, pslot(off), AF.Silu)
                    i1 = s.activation(ATM[:, 0:BLK], pslot(off), AF.Silu)
                    SE.inc(i1, 1)
                blk.scalar(_s)
                se_n = SE.n

                def _t(t, n=n, se_n=se_n):
                    t.wait_ge(SE.h, se_n)
                    t.wait_ge(EV.h, EV.n)
                    for j in range(NT):
                        it = t.transpose(TP[:, j * 128:(j + 1) * 128],
                                         ATM[:, j * 128:(j + 1) * 128], IDENT[:, :])
                    MM.inc(it, 1)
                blk.tensor(_t)
                mm_t = MM.n

                def _v(v, n=n, mm_t=mm_t):
                    v.wait_ge(MM.h, mm_t)
                    dst = UB_QB[:, :].rearrange("p (j c) -> p j c", j=NT)[:, :, n * 128:(n + 1) * 128]
                    iv = v.tensor_tensor(
                        dst, TP[:, 0:NT * 128].rearrange("p (j c) -> p j c", j=NT),
                        BREP[:, :].rearrange("p (h d) -> p h d", h=H)[:, n:n + 1, :].broadcast_to([128, NT, 128]),
                        ALU.add)
                    EV.inc(iv, 1)
                blk.vector(_v)
            def _plq(s):
                s.preload_act_table(AF.Silu)
            blk.scalar(_plq)
            big_proj(wq_b, BIGA, evac_q, wp_gate=32)

            def evac_k(n, mm_n, off=0):
                def _s(s, n=n, mm_n=mm_n, off=off):
                    s.wait_ge(MM.h, mm_n)
                    i1 = s.copy(ATM[:, 0:BLK], pslot(off))
                    SE.inc(i1, 1)
                blk.scalar(_s)
                se_n = SE.n

                def _t(t, n=n, se_n=se_n):
                    t.wait_ge(SE.h, se_n)
                    t.wait_ge(EV.h, EV.n)
                    for j in range(NT):
                        it = t.transpose(TP[:, j * 128:(j + 1) * 128],
                                         ATM[:, j * 128:(j + 1) * 128], IDENT[:, :])
                    MM.inc(it, 1)
                blk.tensor(_t)
                mm_t = MM.n

                def _sq(s, n=n, mm_t=mm_t):
                    s.wait_ge(MM.h, mm_t)
                    i1 = s.activation(GWB[:, :], TP[:, 0:NT * 128], AF.Square)
                    SE.inc(i1, 1)
                blk.scalar(_sq)
                se_sq = SE.n

                def _v(v, n=n, se_sq=se_sq):
                    v.wait_ge(SE.h, se_sq)
                    iv = v.tensor_reduce(STA[:, 0:NT],
                                         GWB[:, 0:NT * 128].rearrange("p (j d) -> p j d", j=NT),
                                         AX.X, ALU.add)
                    EV.inc(iv, 1)
                blk.vector(_v)
                ev_ks = EV.n

                def _s2(s, n=n, ev_ks=ev_ks):
                    s.wait_ge(EV.h, ev_ks)
                    i1 = s.activation(STA[:, 4:4 + NT], STA[:, 0:NT], AF.Sqrt, bias=EPSA[:, 1:2])
                    SE.inc(i1, 1)
                blk.scalar(_s2)
                se_rk = SE.n

                def _v2(v, n=n, se_rk=se_rk):
                    v.wait_ge(SE.h, se_rk)
                    v.wait_ge(EV.h, ev_eta)
                    i0 = v.reciprocal(STA[:, 8:8 + NT], STA[:, 4:4 + NT])
                    vfence(v, i0)
                    dst_u = UB_U[:, :].rearrange("p (j c) -> p j c", j=NT)[:, :, n * 128:(n + 1) * 128]
                    i0 = v.tensor_tensor(dst_u,
                                         TP[:, 0:NT * 128].rearrange("p (j d) -> p j d", j=NT),
                                         STA[:, 8:8 + NT].unsqueeze(2).broadcast_to([128, NT, DH]),
                                         ALU.mult)
                    vfence(v, i0)
                    dst_ke = UB_KE[:, :].rearrange("p (j c) -> p j c", j=NT)[:, :, n * 128:(n + 1) * 128]
                    iv = v.tensor_tensor(dst_ke, dst_u,
                                         ETAN[:, :].rearrange("p (j h) -> p j h", j=NT)[
                                             :, :, n:n + 1].broadcast_to([128, NT, DH]),
                                         ALU.mult)
                    EV.inc(iv, 1)
                blk.vector(_v2)
                ev_kn = EV.n

                def _t2(t, n=n, ev_kn=ev_kn):
                    t.wait_ge(EV.h, ev_kn)
                    for j in range(NT):
                        it = t.transpose(TP[:, 512 + j * 128:512 + (j + 1) * 128],
                                         UB_U[:, j * D + n * 128:j * D + (n + 1) * 128],
                                         IDENT[:, :])
                    MM.inc(it, 1)
                blk.tensor(_t2)
                mm_t2 = MM.n

                def _v3(v, n=n, mm_t2=mm_t2):
                    v.wait_ge(MM.h, mm_t2)
                    dstk = BIGB[:, n * 2 * BLK:(n + 1) * 2 * BLK].rearrange(
                        "p (j c x t) -> p j c x t", j=NT, c=8, x=2)[:, :, :, 0, :]
                    v.tensor_copy(dstk,
                                  TP[:, 512:1024].rearrange("p (j c t) -> p j c t", j=NT, c=8))
                    iv = v.tensor_copy(KSC[:, n * BLK:(n + 1) * BLK], TP[:, 512:1024])
                    EV.inc(iv, 1)
                blk.vector(_v3)
            def _plk(s):
                s.preload_act_table(AF.Sqrt)
            blk.scalar(_plk)
            big_proj(wk_b, BIGA, evac_k, wp_gate=48)
            barrier()

            def evac_v(n, mm_n, off=0):
                def _s(s, n=n, mm_n=mm_n, off=off):
                    s.wait_ge(MM.h, mm_n)
                    s.wait_ge(EV.h, EV.n)
                    i1 = s.copy(ATM[:, 0:BLK], pslot(off))
                    SE.inc(i1, 1)
                blk.scalar(_s)
                se_v = SE.n

                def _t(t, n=n, se_v=se_v):
                    t.wait_ge(SE.h, se_v)
                    t.wait_ge(EV.h, EV.n)
                    for j in range(NT):
                        it = t.transpose(TP[:, j * 128:(j + 1) * 128],
                                         ATM[:, j * 128:(j + 1) * 128], IDENT[:, :])
                    MM.inc(it, 1)
                blk.tensor(_t)
                mm_vt = MM.n

                def _v(v, n=n, mm_vt=mm_vt):
                    v.wait_ge(MM.h, mm_vt)
                    for j in range(NT):
                        i0 = v.tensor_tensor(TMP2[:, j * 128:(j + 1) * 128],
                                             UB_U[:, j * D + n * 128:j * D + (n + 1) * 128],
                                             TP[:, j * 128:(j + 1) * 128], ALU.subtract)
                    vfence(v, i0)
                    for j in range(NT):
                        i0 = v.tensor_tensor(TMP3[:, j * 128:(j + 1) * 128],
                                             TMP2[:, j * 128:(j + 1) * 128],
                                             WREP[:, n * 128:(n + 1) * 128], ALU.mult)
                    vfence(v, i0)
                    iv = None
                    for j in range(NT):
                        iv = v.tensor_tensor(UB_U[:, j * D + n * 128:j * D + (n + 1) * 128],
                                             TMP3[:, j * 128:(j + 1) * 128],
                                             WBREP[:, n * 128:(n + 1) * 128], ALU.add)
                    EV.inc(iv, 1)
                blk.vector(_v)
            big_proj(wv_b, BIGA, evac_v, wp_gate=64)
            barrier()

            ev_pass2 = {}
            sh_vn = {}
            sh_uq = {}
            sh_o = {}
            mm_diag = {}
            ev_vx2d = {}
            state = {"ev_supd": EV.n, "ev_gram": EV.n}
            for l in range(NT):
                sc = b * NT + l

                for hp in range(8):
                    prev_ev = state.get("ev_gpair", EV.n)

                    def _tg(t, l=l, hp=hp, prev_ev=prev_ev):
                        t.wait_ge(EV.h, prev_ev)
                        if hp == 0:
                            t.wait_ge(EV.h, state["ev_supd"])
                        for q in range(2):
                            h = hp * 2 + q
                            it = t.matmul(GD[:, q * 256:(q + 1) * 256],
                                          KSC[:, h * BLK + l * 128:h * BLK + (l + 1) * 128],
                                          BIGB[:, h * 2 * BLK + l * 256:h * 2 * BLK + (l + 1) * 256],
                                          start=True, stop=True)
                        MM.inc(it, 1)
                    blk.tensor(_tg)
                    mm_g = MM.n

                    def _v(v, l=l, hp=hp, mm_g=mm_g):
                        v.wait_ge(MM.h, mm_g)
                        iv = None
                        for q in range(2):
                            h = hp * 2 + q
                            iv = v.scalar_tensor_tensor(GSB[:, h * 256:(h + 1) * 256],
                                                        GD[:, q * 256:(q + 1) * 256],
                                                        ETAN[:, l * H + h:l * H + h + 1],
                                                        GMASK[:, :], ALU.mult, ALU.mult)
                        EV.inc(iv, 1)
                    blk.vector(_v)
                    state["ev_gpair"] = EV.n
                state["ev_gram"] = EV.n
                if dbg and sc == 0:
                    def _gd3(g, evn=EV.n):
                        g.wait_ge(EV.h, evn)
                        dd = g.dma_start(out=dbg_t["d_gsb"][:, :], in_=GSB[:, :])
                        WL.inc(dd, 16)
                    blk.gpsimd(_gd3)

                def _vz(v, sc=sc, mmn=MM.n):
                    v.wait_ge(MM.h, mmn)
                    iv = v.memset(VNS[:, :], 0.0)
                    EV.inc(iv, 1)
                blk.vector(_vz)
                state["ev_vnz"] = EV.n

                def _y0(sy, l=l, sc=sc):
                    i0 = sc * 8
                    slot = i0 % 2
                    sy.wait_ge(EV.h, ev_pass2.get(i0 - 2, 0))
                    for h4 in range(4):
                        s_u = UB_U[0:16, :].rearrange("p (j g x d) -> p j g x d",
                                                      j=NT, g=4, x=4)[:, l, :, h4, :]
                        d_u = UQPK[32 * h4:32 * h4 + 16,
                                   slot * 512:(slot + 1) * 512].rearrange("p (g d) -> p g d", g=4)
                        dd = sy.dma_start(out=d_u, in_=s_u)
                        WL.inc(dd, 16)
                        s_q = UB_QB[0:16, :].rearrange("p (j g x d) -> p j g x d",
                                                       j=NT, g=4, x=4)[:, l, :, h4, :]
                        d_q = UQPK[32 * h4 + 16:32 * h4 + 32,
                                   slot * 512:(slot + 1) * 512].rearrange("p (g d) -> p g d", g=4)
                        dd = sy.dma_start(out=d_q, in_=s_q)
                        WL.inc(dd, 16)
                blk.gpsimd(_y0)
                sh_uq[sc * 8] = WL.n

                for il in range(8):
                    i = sc * 8 + il
                    slot = i % 4
                    csl = KQ[:, slot * 512:(slot + 1) * 512]

                    def _yg(sy, i=i, il=il, evg=state["ev_gram"]):
                        sy.wait_ge(EV.h, evg)
                        if i >= 2:
                            sy.wait_ge(MM.h, mm_diag.get(i - 2, 0))
                        gsl = (i % 2) * 128
                        for h4 in range(4):
                            src_ap = GSB[16 * il:16 * il + 16, :].rearrange(
                                "p (g c) -> p g c", g=4)[:, :, h4 * 256 + il * 32:h4 * 256 + (il + 1) * 32]
                            dst_ap = GSDB[32 * h4:32 * h4 + 16, gsl:gsl + 128].rearrange(
                                "p (g c) -> p g c", g=4)
                            dd = sy.dma_start(out=dst_ap, in_=src_ap)
                            WL.inc(dd, 16)
                    blk.gpsimd(_yg)
                    sh_gd = WL.n

                    def _t(t, l=l, il=il, i=i, slot=slot):
                        t.wait_ge(EV.h, ev_pass2.get(i - 4, 0))
                        t.wait_ge(EV.h, state["ev_gram"])
                        t.wait_ge(EV.h, state["ev_supd"])
                        t.wait_ge(EV.h, state["ev_vnz"])
                        for h in range(H):
                            lhs = BIGB[:, h * 2 * BLK + l * 256 + il * 32:h * 2 * BLK + l * 256 + (il + 1) * 32]
                            it = t.matmul(KQ[32 * (h % 4):32 * (h % 4) + 32,
                                             slot * 512 + (h // 4) * 128:slot * 512 + (h // 4) * 128 + 128],
                                          lhs, SB16[:, h * DH:(h + 1) * DH],
                                          start=(h < 4), stop=False,
                                          skip_group_check=True,
                                          tile_position=(0, 32 * (h % 4)))
                        MM.inc(it, 1)
                    blk.tensor(_t)

                    def _tp(t, l=l, il=il, i=i, slot=slot):
                        if il > 0:
                            t.wait_ge(SH.h, sh_vn[i - 1])
                        kpre = 32 * ((il + 1) // 2)
                        if il > 0:
                            for h in range(H):
                                glhs = GSB[0:kpre, h * 256 + il * 32:h * 256 + (il + 1) * 32]
                                it = t.matmul(KQ[32 * (h % 4):32 * (h % 4) + 32,
                                                 slot * 512 + (h // 4) * 128:slot * 512 + (h // 4) * 128 + 128],
                                              glhs, VNS[0:kpre, h * DH:(h + 1) * DH],
                                              start=False, stop=False,
                                              skip_group_check=True,
                                              tile_position=(0, 32 * (h % 4)))
                        else:
                            it = t.nop()
                        MM.inc(it, 1)
                    blk.tensor(_tp)
                    mm_kq = MM.n

                    def _s(s, i=i, csl=csl, mm_kq=mm_kq):
                        s.wait_ge(MM.h, mm_kq)
                        i1 = s.activation(GD[:, :], csl, AF.Square)
                        SE.inc(i1, 1)
                    blk.scalar(_s)
                    se_sq1 = SE.n

                    def _v(v, i=i, csl=csl, se_sq1=se_sq1):
                        v.wait_ge(SE.h, se_sq1)
                        v.tensor_reduce(STA[:, 4:8], csl.rearrange("p (g d) -> p g d", g=4),
                                        AX.X, ALU.add)
                        v.tensor_reduce(STA[:, 0:4], GD[:, :].rearrange("p (g d) -> p g d", g=4),
                                        AX.X, ALU.add)
                        i0 = v.tensor_tensor(STA[:, 8:12], STA[:, 4:8], STA[:, 4:8], ALU.mult)
                        vfence(v, i0)
                        iv = v.scalar_tensor_tensor(STA[:, 12:16], STA[:, 8:12], -1.0 / DH,
                                                    STA[:, 0:4], ALU.mult, ALU.add)
                        EV.inc(iv, 1)
                    blk.vector(_v)
                    ev_vx = EV.n

                    def _s2(s, i=i, ev_vx=ev_vx):
                        s.wait_ge(EV.h, ev_vx)
                        i3 = s.activation(STA[:, 16:20], STA[:, 12:16], AF.Sqrt,
                                          bias=EPSA[:, 1:2], scale=1.0 / DH)
                        SE.inc(i3, 1)
                    blk.scalar(_s2)
                    se_std = SE.n

                    def _v2(v, i=i, csl=csl, se_std=se_std, sh_uq_n=sh_uq[i]):
                        v.wait_ge(SE.h, se_std)
                        H2 = [dict() for _ in range(2)]
                        for h in range(2):
                            c0 = 256 * h
                            H2[h]["csl"] = csl[:, c0:c0 + 256]
                            H2[h]["K"] = KHB[:, c0:c0 + 256]
                            H2[h]["Ks"] = KHB[:, c0:c0 + 256].rearrange("p (g d) -> p g d", g=2)
                            H2[h]["G"] = GWBB[:, c0:c0 + 256]
                            H2[h]["Gs"] = GWBB[:, c0:c0 + 256].rearrange("p (g d) -> p g d", g=2)
                            H2[h]["O"] = VNPK[:, c0:c0 + 256]
                            H2[h]["W"] = WMIXB[:, c0:c0 + 256]
                            H2[h]["U"] = UQPK[:, (i % 2) * 512 + c0:(i % 2) * 512 + c0 + 256]
                        v.reciprocal(STA[:, 20:24], STA[:, 16:20])
                        v.tensor_scalar_mul(STA[:, 24:28], STA[:, 4:8], 1.0 / DH)
                        v.tensor_copy(STB[:, 0:4], STA[:, 20:24])
                        for h in range(2):
                            v.tensor_tensor(H2[h]["Ks"],
                                            H2[h]["csl"].rearrange("p (g d) -> p g d", g=2),
                                            STA[:, 24 + 2 * h:26 + 2 * h].unsqueeze(2).broadcast_to([128, 2, DH]),
                                            ALU.subtract)
                        for h in range(2):
                            v.tensor_tensor(H2[h]["Ks"], H2[h]["Ks"],
                                            STB[:, 2 * h:2 * h + 2].unsqueeze(2).broadcast_to([128, 2, DH]),
                                            ALU.mult)
                        for h in range(2):
                            v.tensor_tensor(H2[h]["G"], H2[h]["K"], H2[h]["W"], ALU.mult)
                        v.wait_ge(WL.h, sh_uq_n)
                        for h in range(2):
                            v.tensor_tensor(H2[h]["G"], H2[h]["G"], H2[h]["U"], ALU.add)
                        for h in range(2):
                            v.tensor_tensor(H2[h]["O"], H2[h]["G"], H2[h]["K"], ALU.mult)
                        v.tensor_reduce(STA[:, 28:32], GWBB[:, :].rearrange("p (g d) -> p g d", g=4),
                                        AX.X, ALU.add)
                        v.tensor_reduce(STA[:, 32:36], VNPK[:, :].rearrange("p (g d) -> p g d", g=4),
                                        AX.X, ALU.add)
                        i0 = v.scalar_tensor_tensor(STA[:, 8:12], STA[:, 28:32], -1.0 / DH,
                                                    STA[:, 20:24], ALU.mult, ALU.mult)
                        i1 = v.scalar_tensor_tensor(STA[:, 36:40], STA[:, 32:36], -1.0 / DH,
                                                    STA[:, 20:24], ALU.mult, ALU.mult)
                        vfence(v, i1)
                        v.tensor_copy(STB[:, 4:8], STA[:, 36:40])
                        v.tensor_copy(STB[:, 8:12], STA[:, 8:12])
                        for h in range(2):
                            v.tensor_tensor(H2[h]["Ks"], H2[h]["Ks"],
                                            STB[:, 4 + 2 * h:6 + 2 * h].unsqueeze(2).broadcast_to([128, 2, DH]),
                                            ALU.mult)
                        for h in range(2):
                            v.tensor_tensor(H2[h]["Gs"], H2[h]["Gs"],
                                            STB[:, 2 * h:2 * h + 2].unsqueeze(2).broadcast_to([128, 2, DH]),
                                            ALU.mult)
                        for h in range(2):
                            v.tensor_tensor(H2[h]["Ks"], H2[h]["Ks"],
                                            STB[:, 8 + 2 * h:10 + 2 * h].unsqueeze(2).broadcast_to([128, 2, DH]),
                                            ALU.add)
                        iv = None
                        for h in range(2):
                            iv = v.tensor_tensor(VNPK[:, 256 * h:256 * h + 256],
                                                 H2[h]["G"], H2[h]["K"], ALU.add)
                        EV.inc(iv, 1)
                    blk.vector(_v2)
                    ev_k = EV.n

                    def _y(sy, i=i, il=il, l=l, ev_k=ev_k):
                        sy.wait_ge(EV.h, ev_k)
                        for h4 in range(4):
                            src = VNPK[32 * h4:32 * h4 + 16, :].rearrange("p (g d) -> p g d", g=4)
                            dst = VNS[16 * il:16 * il + 16, :].rearrange(
                                "p (g x d) -> p g x d", g=4, x=4)[:, :, h4, :]
                            dd = sy.dma_start(out=dst, in_=src)
                            SH.inc(dd, 16)
                    blk.sync(_y)
                    sh_vn[i] = SH.n

                    if il < 7:
                        def _y2(sy, i=i, il=il, l=l):
                            sy.wait_ge(EV.h, ev_pass2.get(i - 1, 0))
                            slot2 = (i + 1) % 2
                            for h4 in range(4):
                                s_u = UB_U[16 * (il + 1):16 * (il + 2), :].rearrange(
                                    "p (j g x d) -> p j g x d", j=NT, g=4, x=4)[:, l, :, h4, :]
                                d_u = UQPK[32 * h4:32 * h4 + 16,
                                           slot2 * 512:(slot2 + 1) * 512].rearrange("p (g d) -> p g d", g=4)
                                dd = sy.dma_start(out=d_u, in_=s_u)
                                WL.inc(dd, 16)
                                s_q = UB_QB[16 * (il + 1):16 * (il + 2), :].rearrange(
                                    "p (j g x d) -> p j g x d", j=NT, g=4, x=4)[:, l, :, h4, :]
                                d_q = UQPK[32 * h4 + 16:32 * h4 + 32,
                                           slot2 * 512:(slot2 + 1) * 512].rearrange("p (g d) -> p g d", g=4)
                                dd = sy.dma_start(out=d_q, in_=s_q)
                                WL.inc(dd, 16)
                        blk.gpsimd(_y2)
                        sh_uq[i + 1] = WL.n

                    def _t2(t, i=i, il=il, slot=slot, ev_k=ev_k, sh_gd=sh_gd):
                        t.wait_ge(EV.h, ev_k)
                        t.wait_ge(WL.h, sh_gd)
                        gsl = (i % 2) * 128
                        for h in range(H):
                            glhs = GSDB[32 * (h % 4):32 * (h % 4) + 16,
                                        gsl + (h // 4) * 32:gsl + (h // 4 + 1) * 32]
                            it = t.matmul(KQ[32 * (h % 4):32 * (h % 4) + 32,
                                             slot * 512 + (h // 4) * 128:slot * 512 + (h // 4) * 128 + 128],
                                          glhs,
                                          VNPK[32 * (h % 4):32 * (h % 4) + 16,
                                               (h // 4) * 128:(h // 4 + 1) * 128],
                                          start=False, stop=False,
                                          skip_group_check=True,
                                          tile_position=(32 * (h % 4), 32 * (h % 4)))
                        MM.inc(it, 1)
                    blk.tensor(_t2)
                    mm_dg = MM.n
                    mm_diag[i] = mm_dg

                    if dbg and b == 0 and i in (0, 1):
                        def _vd(v, i=i, csl=csl, mm_dg=mm_dg):
                            v.wait_ge(MM.h, mm_dg)
                            iv = v.tensor_copy(KHAT[:, :], csl)
                            EV.inc(iv, 1)
                        blk.vector(_vd)
                        ev_dsc = EV.n

                        def _gd(g, i=i, ev_dsc=ev_dsc):
                            g.wait_ge(EV.h, ev_dsc)
                            dd = g.dma_start(out=dbg_t["d_kq%d" % i][:, :], in_=KHAT[:, :])
                            WL.inc(dd, 16)
                            dd = g.dma_start(out=dbg_t["d_vnpk%d" % i][:, :], in_=VNPK[:, :])
                            WL.inc(dd, 16)
                        blk.gpsimd(_gd)

                    def _s5(s, i=i, csl=csl, mm_dg=mm_dg):
                        s.wait_ge(MM.h, mm_dg)
                        i6 = s.activation(GD[:, :], csl, AF.Square)
                        SE.inc(i6, 1)
                    blk.scalar(_s5)
                    se_sq2 = SE.n

                    def _v5(v, i=i, csl=csl, se_sq2=se_sq2):
                        v.wait_ge(SE.h, se_sq2)
                        v.tensor_reduce(STA[:, 4:8], csl.rearrange("p (g d) -> p g d", g=4),
                                        AX.X, ALU.add)
                        v.tensor_reduce(STA[:, 0:4], GD[:, :].rearrange("p (g d) -> p g d", g=4),
                                        AX.X, ALU.add)
                        i0 = v.tensor_tensor(STA[:, 8:12], STA[:, 4:8], STA[:, 4:8], ALU.mult)
                        vfence(v, i0)
                        iv = v.scalar_tensor_tensor(STA[:, 12:16], STA[:, 8:12], -1.0 / DH,
                                                    STA[:, 0:4], ALU.mult, ALU.add)
                        EV.inc(iv, 1)
                    blk.vector(_v5)
                    ev_vx2 = EV.n
                    ev_vx2d[i] = EV.n

                    def _s6(s, i=i, ev_vx2=ev_vx2):
                        s.wait_ge(EV.h, ev_vx2)
                        i7 = s.activation(STA[:, 16:20], STA[:, 12:16], AF.Sqrt,
                                          bias=EPSA[:, 1:2], scale=1.0 / DH)
                        SE.inc(i7, 1)
                    blk.scalar(_s6)
                    se_std2 = SE.n

                    def _v6(v, i=i, csl=csl, se_std2=se_std2,
                            sh_o_n=sh_o.get(i - 1, 0)):
                        v.wait_ge(SE.h, se_std2)
                        v.reciprocal(STA[:, 20:24], STA[:, 16:20])
                        v.tensor_scalar_mul(STA[:, 24:28], STA[:, 4:8], 1.0 / DH)
                        v.tensor_copy(STB[:, 0:4], STA[:, 20:24])
                        for h in range(2):
                            c0 = 256 * h
                            v.tensor_tensor(KHB[:, c0:c0 + 256].rearrange("p (g d) -> p g d", g=2),
                                            csl[:, c0:c0 + 256].rearrange("p (g d) -> p g d", g=2)
                                            if False else
                                            csl.rearrange("p (x gd) -> p x gd", x=2)[:, h, :].rearrange("p (g d) -> p g d", g=2),
                                            STA[:, 24 + 2 * h:26 + 2 * h].unsqueeze(2).broadcast_to([128, 2, DH]),
                                            ALU.subtract)
                        for h in range(2):
                            c0 = 256 * h
                            v.tensor_tensor(KHB[:, c0:c0 + 256].rearrange("p (g d) -> p g d", g=2),
                                            KHB[:, c0:c0 + 256].rearrange("p (g d) -> p g d", g=2),
                                            STB[:, 2 * h:2 * h + 2].unsqueeze(2).broadcast_to([128, 2, DH]),
                                            ALU.mult)
                        for h in range(2):
                            c0 = 256 * h
                            v.tensor_tensor(GWBB[:, c0:c0 + 256], KHB[:, c0:c0 + 256],
                                            WMIXB[:, c0:c0 + 256], ALU.mult)
                        if sh_o_n:
                            v.wait_ge(SH.h, sh_o_n)
                        iv = None
                        for h in range(2):
                            c0 = 256 * h
                            iv = v.tensor_tensor(OPK[:, c0:c0 + 256], GWBB[:, c0:c0 + 256],
                                                 UQPK[:, (i % 2) * 512 + c0:(i % 2) * 512 + c0 + 256],
                                                 ALU.add)
                        EV.inc(iv, 1)
                    blk.vector(_v6)
                    ev_pass2[i] = EV.n

                    def _y3(sy, i=i, il=il, l=l):
                        sy.wait_ge(EV.h, ev_pass2[i])
                        for h4 in range(4):
                            src = OPK[32 * h4 + 16:32 * h4 + 32, :].rearrange("p (g d) -> p g d", g=4)
                            dst = BIGA[16 * il:16 * il + 16, :].rearrange(
                                "p (j g x d) -> p j g x d", j=NT, g=4, x=4)[:, l, :, h4, :]
                            dd = sy.dma_start(out=dst, in_=src)
                            SH.inc(dd, 16)
                    blk.sync(_y3)
                    sh_o[i] = SH.n

                if dbg and sc == 0:
                    def _gd4(g, shn=sh_vn[sc * 8 + 7]):
                        g.wait_ge(SH.h, shn)
                        dd = g.dma_start(out=dbg_t["d_vns"][:, :], in_=VNS[:, :])
                        WL.inc(dd, 16)
                    blk.gpsimd(_gd4)

                for w in range(4):
                    prev = state.get("ev_supd_w", EV.n)

                    def _t(t, l=l, w=w, prev=prev, sc=sc, evp=ev_vx2d[sc * 8 + 7]):
                        t.wait_ge(EV.h, prev)
                        t.wait_ge(EV.h, evp)
                        t.wait_ge(SH.h, sh_vn[sc * 8 + 7])
                        t.wait_ge(SE.h, SE.n)
                        for hh in range(4):
                            h = w * 4 + hh
                            it = t.matmul(GD[:, hh * 128:(hh + 1) * 128],
                                          UB_KE[:, l * D + h * DH:l * D + (h + 1) * DH],
                                          VNS[:, h * DH:(h + 1) * DH],
                                          start=True, stop=True)
                        MM.inc(it, 1)
                    blk.tensor(_t)
                    mm_ds = MM.n

                    def _v(v, w=w, mm_ds=mm_ds):
                        v.wait_ge(MM.h, mm_ds)
                        i0 = v.tensor_add(S32[:, 512 * w:512 * (w + 1)],
                                          S32[:, 512 * w:512 * (w + 1)], GD[:, :])
                        vfence(v, i0)
                        iv = v.tensor_copy(SB16[:, 512 * w:512 * (w + 1)],
                                           S32[:, 512 * w:512 * (w + 1)])
                        EV.inc(iv, 1)
                    blk.vector(_v)
                    state["ev_supd_w"] = EV.n
                state["ev_supd"] = EV.n

            barrier()
            if dbg and b == 0:
                def _yd(sy):
                    dd = sy.dma_start(out=dbg_t["d_ot"][:, :], in_=BIGA[:, 0:NT * D])
                    WL.inc(dd, 16)
                    dd = sy.dma_start(out=dbg_t["d_s32"][:, :], in_=S32[:, :])
                    WL.inc(dd, 16)
                blk.gpsimd(_yd)

            def _plg(s):
                s.preload_act_table(AF.Sigmoid)
            blk.scalar(_plg)

            def _g2(g):
                g.wait_ge(WP.h, 80)
                g.wait_ge(MM.h, MM.n)
                dw = g.dma_start(out=WSTR[:, 0:D], in_=wg2_b[:, :])
                WH.inc(dw, 16)
            blk.sync(_g2)
            wl_g2 = WH.n
            for j in range(NT):
                for half in range(2):
                    def _t(t, j=j, half=half, wl_g2=wl_g2):
                        t.wait_ge(WH.h, wl_g2)
                        t.wait_ge(SE.h, SE.n)
                        for nn in range(2):
                            it = t.matmul(PP[:, nn * 512:(nn + 1) * 512],
                                          G1F[:, j * 128:(j + 1) * 128],
                                          WSTR[:, half * 1024 + nn * 512:half * 1024 + (nn + 1) * 512],
                                          start=True, stop=True)
                        MM.inc(it, 1)
                    blk.tensor(_t)
                    mm_g2 = MM.n

                    def _s(s, j=j, half=half, mm_g2=mm_g2):
                        s.wait_ge(MM.h, mm_g2)
                        i1 = s.activation(SGB[:, j * D + half * 1024:j * D + (half + 1) * 1024],
                                          PP[:, :], AF.Sigmoid)
                        SE.inc(i1, 1)
                    blk.scalar(_s)
            barrier()
            if dbg and b == 0:
                def _yd(sy):
                    for nm, buf in [("d_kqf", BIGB), ("d_u", UB_U), ("d_qb", UB_QB),
                                    ("d_ke", UB_KE), ("d_sg", SGB)]:
                        dd = sy.dma_start(out=dbg_t[nm][:, :], in_=buf[:, :])
                        WL.inc(dd, 16)
                blk.gpsimd(_yd)

            def _plo(s):
                s.preload_act_table(AF.Sqrt)
            blk.scalar(_plo)
            for j in range(NT):
                def _v(v, j=j):
                    v.wait_ge(EV.h, EV.n)
                    iv = v.tensor_tensor(TMP2[:, :], BIGA[:, j * D:(j + 1) * D],
                                         SGB[:, j * D:(j + 1) * D], ALU.mult)
                    EV.inc(iv, 1)
                blk.vector(_v)
                ev_og = EV.n

                def _s(s, j=j, ev_og=ev_og):
                    s.wait_ge(EV.h, ev_og)
                    i1 = s.activation(TMP3[:, :], TMP2[:, :], AF.Square)
                    SE.inc(i1, 1)
                blk.scalar(_s)
                se_og2 = SE.n

                def _v2(v, j=j, se_og2=se_og2):
                    v.wait_ge(SE.h, se_og2)
                    iv = v.tensor_reduce(OGS[:, :], TMP3[:, :].rearrange("p (h d) -> p h d", h=H),
                                         AX.X, ALU.add)
                    EV.inc(iv, 1)
                blk.vector(_v2)
                ev_ogs = EV.n

                def _s2(s, j=j, ev_ogs=ev_ogs):
                    s.wait_ge(EV.h, ev_ogs)
                    i1 = s.activation(STA[:, 0:H], OGS[:, :], AF.Sqrt, bias=EPSA[:, 1:2], scale=1.0 / DH)
                    SE.inc(i1, 1)
                blk.scalar(_s2)
                se_ogr = SE.n

                def _v3(v, j=j, se_ogr=se_ogr):
                    v.wait_ge(SE.h, se_ogr)
                    i0 = v.reciprocal(STA[:, H:2 * H], STA[:, 0:H])
                    vfence(v, i0)
                    i0 = v.tensor_tensor(ATM[:, :].rearrange("p (h d) -> p h d", h=H),
                                         TMP2[:, :].rearrange("p (h d) -> p h d", h=H),
                                         STA[:, H:2 * H].unsqueeze(2).broadcast_to([128, H, DH]),
                                         ALU.mult)
                    vfence(v, i0)
                    iv = v.tensor_tensor(ATM[:, :], ATM[:, :], NWREP[:, :], ALU.mult)
                    EV.inc(iv, 1)
                blk.vector(_v3)
                ev_o2 = EV.n

                for half in range(2):
                    def mkdst4(j=j, half=half):
                        def dst(v, tp):
                            return v.tensor_copy(
                                BIGB[:, 0:KT * BLK].rearrange("p (k t) -> p k t", k=KT)[
                                    :, half * 8:half * 8 + 8, j * 128:(j + 1) * 128],
                                tp.rearrange("p (c t) -> p c t", c=8))
                        return dst
                    tp_transposes([ATM[:, (half * 8 + c) * 128:(half * 8 + c + 1) * 128]
                                   for c in range(8)], mkdst4(), wait=[(EV, ev_o2)])
            barrier()
            if dbg and b == 0:
                def _yd(sy):
                    dd = sy.dma_start(out=dbg_t["d_o2f"][:, :], in_=BIGB[:, 0:KT * BLK])
                    WL.inc(dd, 16)
                blk.gpsimd(_yd)

            def evac_wo(n, mm_n, off=0):
                def _s(s, n=n, mm_n=mm_n, off=off):
                    s.wait_ge(MM.h, mm_n)
                    i1 = s.copy(BIGA[:, n * BLK:(n + 1) * BLK], pslot(off))
                    SE.inc(i1, 1)
                blk.scalar(_s)
            big_proj(wo_b, BIGB, evac_wo, wp_gate=96)
            barrier()
            if dbg and b == 0:
                def _yd(sy):
                    dd = sy.dma_start(out=dbg_t["d_attnf"][:, :], in_=BIGA[:, :])
                    WL.inc(dd, 16)
                blk.gpsimd(_yd)

            for j in range(NT):
                for half in range(2):
                    def mkdst5(j=j, half=half):
                        def dst(v, tp):
                            return v.tensor_copy(
                                ATM[:, half * 1024:(half + 1) * 1024].rearrange(
                                    "p (c t) -> p c t", c=8),
                                tp.rearrange("p (c t) -> p c t", c=8))
                        return dst
                    tp_transposes([BIGA[:, (half * 8 + c) * BLK + j * 128:
                                        (half * 8 + c) * BLK + (j + 1) * 128]
                                   for c in range(8)], mkdst5())
                ev_at = EV.n

                def _y(sy, j=j):
                    sy.wait_ge(EV.h, EV.n)
                    dx = sy.dma_start(out=XT[:, :], in_=xin[t0 + j * 128:t0 + (j + 1) * 128, :])
                    LD.inc(dx, 16)
                blk.sync(_y)
                ld_x2 = LD.n

                def _s(s, j=j, ev_at=ev_at):
                    s.wait_ge(EV.h, ev_at)
                    s.wait_ge(EV.h, EV.n)
                    s.activation(PP[:, :], ATM[:, 0:1024], AF.Square, accum_out=STA[:, 16:17])
                    i2 = s.activation(KQ[:, 0:1024], ATM[:, 1024:2048], AF.Square, accum_out=STA[:, 17:18])
                    SE.inc(i2, 1)
                blk.scalar(_s)
                se_sq = SE.n

                def _v(v, j=j, ev_at=ev_at, se_sq=se_sq):
                    v.wait_ge(EV.h, ev_at)
                    v.tensor_reduce(STA[:, 18:19], ATM[:, :], AX.X, ALU.add)
                    v.wait_ge(SE.h, se_sq)
                    v.tensor_add(STA[:, 19:20], STA[:, 16:17], STA[:, 17:18])
                    i0 = v.tensor_tensor(STA[:, 20:21], STA[:, 18:19], STA[:, 18:19], ALU.mult)
                    vfence(v, i0)
                    iv = v.scalar_tensor_tensor(STA[:, 21:22], STA[:, 20:21], -1.0 / D,
                                                STA[:, 19:20], ALU.mult, ALU.add)
                    EV.inc(iv, 1)
                blk.vector(_v)
                ev_vx = EV.n

                def _s2(s, j=j, ev_vx=ev_vx):
                    s.wait_ge(EV.h, ev_vx)
                    i3 = s.activation(STA[:, 22:23], STA[:, 21:22], AF.Sqrt,
                                      bias=EPSA[:, 0:1], scale=1.0 / D)
                    SE.inc(i3, 1)
                blk.scalar(_s2)
                se_std = SE.n

                def _v2(v, j=j, se_std=se_std, ld_x2=ld_x2):
                    v.wait_ge(SE.h, se_std)
                    v.wait_ge(LD.h, ld_x2)
                    v.reciprocal(STA[:, 23:24], STA[:, 22:23])
                    i0 = v.tensor_scalar_mul(STA[:, 24:25], STA[:, 18:19], 1.0 / D)
                    vfence(v, i0)
                    i0 = v.tensor_scalar(TMP2[:, :], ATM[:, :], STA[:, 24:25], STA[:, 23:24],
                                         ALU.subtract, ALU.mult)
                    vfence(v, i0)
                    iv = v.tensor_tensor(XT[:, :], XT[:, :], TMP2[:, :], ALU.add)
                    EV.inc(iv, 1)
                blk.vector(_v2)
                ev_x2 = EV.n

                def _y2(sy, j=j, ev_x2=ev_x2):
                    sy.wait_ge(EV.h, ev_x2)
                    dd = sy.dma_start(out=out[t0 + j * 128:t0 + (j + 1) * 128, :], in_=XT[:, :])
                    SO.inc(dd, 16)
                blk.gpsimd(_y2)

                def _s3(s, j=j, ev_x2=ev_x2):
                    s.wait_ge(EV.h, ev_x2)
                    s.activation(PP[:, :], XT[:, 0:1024], AF.Square, accum_out=STA[:, 16:17])
                    i2 = s.activation(KQ[:, 0:1024], XT[:, 1024:2048], AF.Square, accum_out=STA[:, 17:18])
                    SE.inc(i2, 1)
                blk.scalar(_s3)
                se_sq3 = SE.n

                def _v3(v, j=j, ev_x2=ev_x2, se_sq3=se_sq3):
                    v.wait_ge(EV.h, ev_x2)
                    v.tensor_reduce(STA[:, 18:19], XT[:, :], AX.X, ALU.add)
                    v.wait_ge(SE.h, se_sq3)
                    v.tensor_add(STA[:, 19:20], STA[:, 16:17], STA[:, 17:18])
                    i0 = v.tensor_tensor(STA[:, 20:21], STA[:, 18:19], STA[:, 18:19], ALU.mult)
                    vfence(v, i0)
                    iv = v.scalar_tensor_tensor(STA[:, 21:22], STA[:, 20:21], -1.0 / D,
                                                STA[:, 19:20], ALU.mult, ALU.add)
                    EV.inc(iv, 1)
                blk.vector(_v3)
                ev_vx3 = EV.n

                def _s4(s, j=j, ev_vx3=ev_vx3):
                    s.wait_ge(EV.h, ev_vx3)
                    i3 = s.activation(STA[:, 22:23], STA[:, 21:22], AF.Sqrt,
                                      bias=EPSA[:, 0:1], scale=1.0 / D)
                    SE.inc(i3, 1)
                blk.scalar(_s4)
                se_std3 = SE.n

                def _v4(v, j=j, se_std3=se_std3):
                    v.wait_ge(SE.h, se_std3)
                    v.reciprocal(STA[:, 23:24], STA[:, 22:23])
                    i0 = v.tensor_scalar_mul(STA[:, 24:25], STA[:, 18:19], 1.0 / D)
                    vfence(v, i0)
                    iv = v.tensor_scalar(ATM[:, :], XT[:, :], STA[:, 24:25], STA[:, 23:24],
                                         ALU.subtract, ALU.mult)
                    EV.inc(iv, 1)
                blk.vector(_v4)
                ev_x3h = EV.n

                for half in range(2):
                    def mkdst6(j=j, half=half):
                        def dst(v, tp):
                            return v.tensor_copy(
                                BIGB[:, 0:KT * BLK].rearrange("p (k t) -> p k t", k=KT)[
                                    :, half * 8:half * 8 + 8, j * 128:(j + 1) * 128],
                                tp.rearrange("p (c t) -> p c t", c=8))
                        return dst
                    tp_transposes([ATM[:, (half * 8 + c) * 128:(half * 8 + c + 1) * 128]
                                   for c in range(8)], mkdst6(), wait=[(EV, ev_x3h)])
            barrier()
            if dbg and b == 0:
                def _yd(sy):
                    dd = sy.dma_start(out=dbg_t["d_x3f"][:, :], in_=BIGB[:, 0:KT * BLK])
                    WL.inc(dd, 16)
                blk.gpsimd(_yd)

            def _g(g):
                g.wait_ge(WP.h, 112)
                g.wait_ge(MM.h, MM.n)
                dw = g.dma_start(out=WSTR[:, 0:KT * 8], in_=gwT_b[:, :])
                WH.inc(dw, 16)
            blk.sync(_g)
            wl_gw = WH.n

            def _t(t, wl_gw=wl_gw):
                t.wait_ge(WH.h, wl_gw)
                for k in range(KT):
                    it = t.matmul(GD[0:8, 0:512], WSTR[:, k * 8:(k + 1) * 8],
                                  BIGB[:, k * BLK:(k + 1) * BLK],
                                  start=(k == 0), stop=(k == KT - 1))
                MM.inc(it, 1)
            blk.tensor(_t)
            mm_lg = MM.n

            def _v(v, mm_lg=mm_lg):
                v.wait_ge(MM.h, mm_lg)
                iv = v.tensor_copy(LGT[0:8, :], GD[0:8, 0:BLK])
                EV.inc(iv, 1)
            blk.vector(_v)
            ev_lg = EV.n

            def _t3(t, ev_lg=ev_lg):
                t.wait_ge(EV.h, ev_lg)
                for j in range(NT):
                    it = t.transpose(GD[:, j * 8:j * 8 + 8].bitcast(F32) if False else GD[:, j * 8:j * 8 + 8],
                                     LGT[0:8, j * 128:(j + 1) * 128], IDF32[0:8, 0:8])
                MM.inc(it, 1)
            blk.tensor(_t3)
            mm_lt = MM.n

            def _v2(v, mm_lt=mm_lt):
                v.wait_ge(MM.h, mm_lt)
                iv = v.tensor_copy(LTM[:, :], GD[:, 0:NT * 8])
                EV.inc(iv, 1)
            blk.vector(_v2)
            ev_ltm = EV.n

            def _v4(v, ev_ltm=ev_ltm):
                v.wait_ge(EV.h, ev_ltm)
                for j in range(NT):
                    i0 = v.tensor_reduce(STA[:, j:j + 1], LTM[:, j * 8:(j + 1) * 8],
                                         AX.X, ALU.max)
                vfence(v, i0)
                for j in range(NT):
                    i0 = v.tensor_scalar(CW1[:, j * 8:(j + 1) * 8], LTM[:, j * 8:(j + 1) * 8],
                                         STA[:, j:j + 1], None, ALU.is_ge, ALU.bypass)
                vfence(v, i0)
                for j in range(NT):
                    i0 = v.scalar_tensor_tensor(CW1[:, j * 8:(j + 1) * 8],
                                                CW1[:, j * 8:(j + 1) * 8], -1e30,
                                                LTM[:, j * 8:(j + 1) * 8], ALU.mult, ALU.add)
                vfence(v, i0)
                for j in range(NT):
                    i0 = v.tensor_reduce(STA[:, 4 + j:5 + j], CW1[:, j * 8:(j + 1) * 8],
                                         AX.X, ALU.max)
                vfence(v, i0)
                for j in range(NT):
                    v.tensor_scalar(CW1[:, j * 8:(j + 1) * 8], LTM[:, j * 8:(j + 1) * 8],
                                    STA[:, 4 + j:5 + j], None, ALU.is_ge, ALU.bypass)
                for j in range(NT):
                    iv = v.tensor_scalar(LTM[:, j * 8:(j + 1) * 8], LTM[:, j * 8:(j + 1) * 8],
                                         STA[:, j:j + 1], None, ALU.subtract, ALU.bypass)
                    EV.inc(iv, 1) if j == NT - 1 else None
            blk.vector(_v4)
            ev_cwpre = EV.n

            def _s5(s, ev_cwpre=ev_cwpre):
                s.wait_ge(EV.h, ev_cwpre)
                s.preload_act_table(AF.Exp)
                i1 = s.activation(LTM[:, :], LTM[:, :], AF.Exp)
                SE.inc(i1, 1)
            blk.scalar(_s5)
            se_exp = SE.n

            def _v5(v, se_exp=se_exp):
                v.wait_ge(SE.h, se_exp)
                for j in range(NT):
                    i0 = v.tensor_tensor(LTM[:, j * 8:(j + 1) * 8], LTM[:, j * 8:(j + 1) * 8],
                                         CW1[:, j * 8:(j + 1) * 8], ALU.mult)
                vfence(v, i0)
                for j in range(NT):
                    i0 = v.tensor_reduce(STA[:, 8 + j:9 + j], LTM[:, j * 8:(j + 1) * 8],
                                         AX.X, ALU.add)
                vfence(v, i0)
                for j in range(NT):
                    i0 = v.reciprocal(STA[:, 12 + j:13 + j], STA[:, 8 + j:9 + j])
                vfence(v, i0)
                iv = None
                for j in range(NT):
                    iv = v.tensor_scalar(CWB[:, j * 8:(j + 1) * 8], LTM[:, j * 8:(j + 1) * 8],
                                         STA[:, 12 + j:13 + j], 16.0, ALU.mult, ALU.mult)
                EV.inc(iv, 1)
            blk.vector(_v5)
            ev_cwb = EV.n

            def _t4(t, ev_cwb=ev_cwb):
                t.wait_ge(EV.h, ev_cwb)
                t.wait_ge(EV.h, EV.n)
                for j in range(NT):
                    it = t.transpose(TP[0:8, j * 128:j * 128 + 128],
                                     CWB[:, j * 8:(j + 1) * 8], IDENT[:, :])
                MM.inc(it, 1)
            blk.tensor(_t4)
            mm_cwt = MM.n

            def _v6(v, mm_cwt=mm_cwt):
                v.wait_ge(MM.h, mm_cwt)
                iv = v.tensor_copy(CWF[0:8, :], TP[0:8, 0:NT * 128])
                EV.inc(iv, 1)
            blk.vector(_v6)
            ev_cwf = EV.n

            def _t5(t, ev_cwf=ev_cwf):
                t.wait_ge(EV.h, ev_cwf)
                t.wait_ge(EV.h, EV.n)
                it = t.matmul(PP[:, 0:512], EMAT[0:8, :], CWF[0:8, 0:512], start=True, stop=True)
                MM.inc(it, 1)
            blk.tensor(_t5)
            mm_cwx = MM.n

            def _v7(v, mm_cwx=mm_cwx):
                v.wait_ge(MM.h, mm_cwx)
                iv = v.tensor_copy(CWX[:, :], PP[:, 0:BLK])
                EV.inc(iv, 1)
            blk.vector(_v7)
            ev_cwx = EV.n
            if dbg and b == 0:
                def _yd(sy, ev_cwf=ev_cwf):
                    sy.wait_ge(EV.h, ev_cwf)
                    dd = sy.dma_start(out=dbg_t["d_cwf"][:, :], in_=CWF[0:8, :])
                    WL.inc(dd, 16)
                blk.gpsimd(_yd)

            def _g2(g):
                g.wait_ge(WP.h, 128)
                g.wait_ge(MM.h, MM.n)
                dw = g.dma_start(out=WSTR[:, D:D + KT * 128], in_=a_all_b[:, :])
                WH.inc(dw, 16)
            blk.sync(_g2)
            wl_a = WH.n

            def _t6(t, wl_a=wl_a, ev_cwx=ev_cwx):
                t.wait_ge(WH.h, wl_a)
                t.wait_ge(EV.h, ev_cwx)
                for k in range(KT):
                    it = t.matmul(PP[:, 0:512], WSTR[:, D + k * 128:D + (k + 1) * 128],
                                  BIGB[:, k * BLK:(k + 1) * BLK],
                                  start=(k == 0), stop=(k == KT - 1))
                MM.inc(it, 1)
            blk.tensor(_t6)
            mm_z = MM.n

            def _v8(v, mm_z=mm_z, ev_cwx=ev_cwx):
                v.wait_ge(MM.h, mm_z)
                v.wait_ge(EV.h, ev_cwx)
                iv = v.tensor_tensor(ZCW[:, :], PP[:, 0:BLK], CWX[:, :], ALU.mult)
                EV.inc(iv, 1)
            blk.vector(_v8)
            ev_zcw = EV.n
            do_block._ffn_hist = {0: mm_z, 1: mm_z}
            if dbg and b == 0:
                def _yd(sy, ev_zcw=ev_zcw):
                    sy.wait_ge(EV.h, ev_zcw)
                    dd = sy.dma_start(out=dbg_t["d_zcw"][:, :], in_=ZCW[:, :])
                    WL.inc(dd, 16)
                blk.gpsimd(_yd)

            ffn_pp_hist = {}
            def evac_ffn(n, mm_n, off=0):
                def _s(s, n=n, mm_n=mm_n, off=off):
                    s.wait_ge(MM.h, mm_n)
                    i1 = s.copy(BIGA[:, n * BLK:(n + 1) * BLK], pslot(off))
                    SE.inc(i1, 1)
                blk.scalar(_s)

            for n in range(KT):
                slot = n % 2
                mm_prev2 = getattr(do_block, "_ffn_hist", {}).get(slot, 0)

                def _g3(g, n=n, slot=slot, mm_prev2=mm_prev2):
                    g.wait_ge(WP.h, 144)
                    g.wait_ge(MM.h, mm_prev2)
                    dw = g.dma_start(out=WSTR[:, slot * D:(slot + 1) * D],
                                     in_=wfT_b[:, n * D:(n + 1) * D])
                    WH.inc(dw, 16)
                blk.sync(_g3)
                wl_n = WH.n

                def _t7(t, n=n, slot=slot, wl_n=wl_n, ev_zcw=ev_zcw,
                        se_prev=ffn_pp_hist.get(n % 6, 0)):
                    t.wait_ge(WH.h, wl_n)
                    t.wait_ge(EV.h, ev_zcw)
                    if se_prev:
                        t.wait_ge(SE.h, se_prev)
                    for k in range(KT):
                        t.matmul(pslot(n % 6),
                                 WSTR[:, slot * D + k * 128:slot * D + (k + 1) * 128],
                                 BIGB[:, k * BLK:(k + 1) * BLK],
                                 start=(k == 0), stop=False)
                    it = t.matmul(pslot(n % 6),
                                  BALL[:, n * 128:(n + 1) * 128],
                                  ZCW[:, :], start=False, stop=True)
                    MM.inc(it, 1)
                blk.tensor(_t7)
                hist = getattr(do_block, "_ffn_hist", {})
                hist[slot] = MM.n
                do_block._ffn_hist = hist
                evac_ffn(n, MM.n, n % 6)
                ffn_pp_hist[n % 6] = SE.n
            barrier()

            def _pl4(s):
                s.preload_act_table(AF.Sqrt)
            blk.scalar(_pl4)
            for j in range(NT):
                for half in range(2):
                    def mkdst7(j=j, half=half):
                        def dst(v, tp):
                            return v.tensor_copy(
                                ATM[:, half * 1024:(half + 1) * 1024].rearrange(
                                    "p (c t) -> p c t", c=8),
                                tp.rearrange("p (c t) -> p c t", c=8))
                        return dst
                    tp_transposes([BIGA[:, (half * 8 + c) * BLK + j * 128:
                                        (half * 8 + c) * BLK + (j + 1) * 128]
                                   for c in range(8)], mkdst7())
                ev_ft = EV.n

                so_before = SO.n

                def _y(sy, j=j, so_before=so_before):
                    sy.wait_ge(SO.h, so_before)
                    sy.wait_ge(EV.h, EV.n)
                    dx = sy.dma_start(out=XT[:, :], in_=out[t0 + j * 128:t0 + (j + 1) * 128, :])
                    LD.inc(dx, 16)
                blk.sync(_y)
                ld_x2r = LD.n

                def _s(s, j=j, ev_ft=ev_ft):
                    s.wait_ge(EV.h, ev_ft)
                    s.wait_ge(EV.h, EV.n)
                    s.activation(PP[:, :], ATM[:, 0:1024], AF.Square, accum_out=STA[:, 16:17])
                    i2 = s.activation(KQ[:, 0:1024], ATM[:, 1024:2048], AF.Square, accum_out=STA[:, 17:18])
                    SE.inc(i2, 1)
                blk.scalar(_s)
                se_sq4 = SE.n

                def _v(v, j=j, ev_ft=ev_ft, se_sq4=se_sq4):
                    v.wait_ge(EV.h, ev_ft)
                    v.tensor_reduce(STA[:, 18:19], ATM[:, :], AX.X, ALU.add)
                    v.wait_ge(SE.h, se_sq4)
                    v.tensor_add(STA[:, 19:20], STA[:, 16:17], STA[:, 17:18])
                    i0 = v.tensor_tensor(STA[:, 20:21], STA[:, 18:19], STA[:, 18:19], ALU.mult)
                    vfence(v, i0)
                    iv = v.scalar_tensor_tensor(STA[:, 21:22], STA[:, 20:21], -1.0 / D,
                                                STA[:, 19:20], ALU.mult, ALU.add)
                    EV.inc(iv, 1)
                blk.vector(_v)
                ev_vx4 = EV.n

                def _s2(s, j=j, ev_vx4=ev_vx4):
                    s.wait_ge(EV.h, ev_vx4)
                    i3 = s.activation(STA[:, 22:23], STA[:, 21:22], AF.Sqrt,
                                      bias=EPSA[:, 0:1], scale=1.0 / D)
                    SE.inc(i3, 1)
                blk.scalar(_s2)
                se_std4 = SE.n

                def _v2(v, j=j, se_std4=se_std4, ld_x2r=ld_x2r):
                    v.wait_ge(SE.h, se_std4)
                    v.wait_ge(LD.h, ld_x2r)
                    v.reciprocal(STA[:, 23:24], STA[:, 22:23])
                    i0 = v.tensor_scalar_mul(STA[:, 24:25], STA[:, 18:19], 1.0 / D)
                    vfence(v, i0)
                    i0 = v.tensor_scalar(TMP2[:, :], ATM[:, :], STA[:, 24:25], STA[:, 23:24],
                                         ALU.subtract, ALU.mult)
                    vfence(v, i0)
                    iv = v.tensor_tensor(XT[:, :], XT[:, :], TMP2[:, :], ALU.add)
                    EV.inc(iv, 1)
                blk.vector(_v2)
                ev_x3 = EV.n

                def _y2(sy, j=j, ev_x3=ev_x3):
                    sy.wait_ge(EV.h, ev_x3)
                    dd = sy.dma_start(out=out[t0 + j * 128:t0 + (j + 1) * 128, :], in_=XT[:, :])
                    SO.inc(dd, 16)
                blk.gpsimd(_y2)
            barrier()

        for b in range(NBLK):
            do_block(b)

        def _fin(sy):
            sy.wait_ge(SO.h, SO.n)
        blk.sync(_fin)

    return nc


_NC_CACHE = {}


def kernel(**inputs):
    T = 2048
    x = np.ascontiguousarray(np.asarray(inputs["L0_attn_x"], dtype=np.float32))
    def colblock(w, ncols=128):
        Din, Dout = w.shape
        nk, nn = Din // 128, Dout // ncols
        return np.ascontiguousarray(
            w.reshape(nk, 128, nn, ncols).transpose(1, 2, 0, 3).reshape(128, nn * nk * ncols))

    base = {
        "wq": colblock(np.asarray(inputs["L0_attn_Wq"], dtype=np.float32)),
        "wk": colblock(np.asarray(inputs["L0_attn_Wk"], dtype=np.float32)),
        "wv": colblock(np.asarray(inputs["L0_attn_Wv"], dtype=np.float32)),
        "wo": colblock(np.asarray(inputs["L0_attn_Wo"], dtype=np.float32)),
        "wbg": colblock(np.concatenate(
            [np.asarray(inputs["L0_attn_Wg1"], dtype=np.float32),
             np.asarray(inputs["L0_attn_Wbeta"], dtype=np.float32)], axis=1), ncols=144),
        "wg2": np.ascontiguousarray(np.asarray(inputs["L0_attn_Wg2"], dtype=np.float32)),
        "lnw": np.ascontiguousarray(np.asarray(inputs["L0_attn_ln_w"], dtype=np.float32)),
        "lnb": np.ascontiguousarray(np.asarray(inputs["L0_attn_ln_b"], dtype=np.float32)),
        "nrmw": np.ascontiguousarray(np.asarray(inputs["L0_attn_norm_w"], dtype=np.float32)),
        "wfT": colblock(np.ascontiguousarray(np.asarray(inputs["L0_ffn_weight"], dtype=np.float32).T)),
        "gwT": colblock(np.ascontiguousarray(np.asarray(inputs["L0_ffn_gate_w"], dtype=np.float32).T), ncols=8),
        "a_all": colblock(np.ascontiguousarray(np.asarray(
            inputs["L0_ffn_A_w"], dtype=np.float32).transpose(2, 0, 1).reshape(D, 128))),
        "b_all": np.ascontiguousarray(np.asarray(
            inputs["L0_ffn_B_w"], dtype=np.float32).transpose(0, 2, 1).reshape(128, D)),
    }
    if "nc" not in _NC_CACHE:
        nc = bass.Bass("TRN2", target_bir_lowering=False, debug=False)
        nc.detect_race_conditions = False
        build(nc, T)
        nc.finalize()
        _NC_CACHE["nc"] = nc
    nc = _NC_CACHE["nc"]
    in_maps = []
    for c in range(8):
        m = dict(base)
        m["x"] = np.ascontiguousarray(x[c])
        in_maps.append(m)
    res = run_bass_kernel_spmd(nc, in_maps, core_ids=list(range(8)))
    return np.stack([r["out"] for r in res.results], axis=0)


# version 18
# speedup vs baseline: 1.0046x; 1.0046x over previous
import contextlib
import numpy as np

import concourse.bass as bass
import concourse.mybir as mybir
from concourse.bass_utils import run_bass_kernel_spmd

F32 = mybir.dt.float32
BF16 = mybir.dt.bfloat16
AF = mybir.ActivationFunctionType
ALU = mybir.AluOpType
AX = mybir.AxisListType

D = 2048
H = 16
DH = 128
CH = 16
KT = 16
EPS_LN = 1e-5
EPS_IN = 1e-6


class Sem:
    def __init__(self, stack, nc, name):
        self.h = stack.enter_context(nc.semaphore(name))
        self.n = 0

    def inc(self, instr, k=1):
        instr.then_inc(self.h, k)
        self.n += k


def build(nc, T, dbg=False):
    BLK = 512
    NBLK = T // BLK
    NT = BLK // 128

    xin = nc.dram_tensor("x", [T, D], F32, kind="ExternalInput").ap()
    wq = nc.dram_tensor("wq", [128, KT * D], F32, kind="ExternalInput").ap()
    wk = nc.dram_tensor("wk", [128, KT * D], F32, kind="ExternalInput").ap()
    wv = nc.dram_tensor("wv", [128, KT * D], F32, kind="ExternalInput").ap()
    wo = nc.dram_tensor("wo", [128, KT * D], F32, kind="ExternalInput").ap()
    wbg = nc.dram_tensor("wbg", [128, KT * 144], F32, kind="ExternalInput").ap()
    wg2 = nc.dram_tensor("wg2", [128, D], F32, kind="ExternalInput").ap()
    lnw = nc.dram_tensor("lnw", [H, DH], F32, kind="ExternalInput").ap()
    lnb = nc.dram_tensor("lnb", [H, DH], F32, kind="ExternalInput").ap()
    nrmw = nc.dram_tensor("nrmw", [D], F32, kind="ExternalInput").ap()
    wfT = nc.dram_tensor("wfT", [128, KT * D], F32, kind="ExternalInput").ap()
    gwT = nc.dram_tensor("gwT", [128, KT * 8], F32, kind="ExternalInput").ap()
    a_all = nc.dram_tensor("a_all", [128, KT * 128], F32, kind="ExternalInput").ap()
    b_all = nc.dram_tensor("b_all", [128, D], F32, kind="ExternalInput").ap()
    out = nc.dram_tensor("out", [T, D], F32, kind="ExternalOutput").ap()
    wq_b = nc.dram_tensor("wq_b", [128, KT * D], BF16, kind="Internal").ap()
    wk_b = nc.dram_tensor("wk_b", [128, KT * D], BF16, kind="Internal").ap()
    wv_b = nc.dram_tensor("wv_b", [128, KT * D], BF16, kind="Internal").ap()
    wo_b = nc.dram_tensor("wo_b", [128, KT * D], BF16, kind="Internal").ap()
    wfT_b = nc.dram_tensor("wfT_b", [128, KT * D], BF16, kind="Internal").ap()
    wbg_b = nc.dram_tensor("wbg_b", [128, KT * 144], BF16, kind="Internal").ap()
    wg2_b = nc.dram_tensor("wg2_b", [128, D], BF16, kind="Internal").ap()
    a_all_b = nc.dram_tensor("a_all_b", [128, KT * 128], BF16, kind="Internal").ap()
    gwT_b = nc.dram_tensor("gwT_b", [128, KT * 8], BF16, kind="Internal").ap()

    dbg_t = {}
    if dbg:
        for nm, shp in [("d_kqf", [128, H * 2 * BLK]), ("d_u", [128, NT * D]),
                        ("d_qb", [128, NT * D]), ("d_ke", [128, NT * D]),
                        ("d_etan", [128, NT * H]), ("d_sg", [128, NT * D]),
                        ("d_ot", [128, NT * D]), ("d_o2f", [128, KT * BLK]),
                        ("d_attnf", [128, KT * BLK]), ("d_x3f", [128, KT * BLK]),
                        ("d_cwf", [8, BLK]), ("d_zcw", [128, BLK]),
                        ("d_s32", [128, H * DH]), ("d_xh1f", [128, KT * BLK]),
                        ("d_gsb", [128, H * 256]), ("d_vns", [128, H * DH]),
                        ("d_vnpk0", [128, 512]), ("d_kq0", [128, 512]),
                        ("d_vnpk1", [128, 512]), ("d_kq1", [128, 512]),
                        ("d_opk0", [128, 512]), ("d_opk1", [128, 512])]:
            dbg_t[nm] = nc.dram_tensor(nm, shp, F32, kind="ExternalOutput").ap()

    st = contextlib.ExitStack()
    with st:
        sbt = lambda name, shape, dt: st.enter_context(nc.sbuf_tensor(name, shape, dt))
        pst = lambda name, shape, dt: st.enter_context(nc.psum_tensor(name, shape, dt))

        BIGA = sbt("BIGA", [128, KT * BLK], BF16)
        BIGB = sbt("BIGB", [128, H * 2 * BLK], BF16)
        SGB = sbt("SGB", [128, NT * D], BF16)
        KSC = SGB
        UB_U = sbt("UB_U", [128, NT * D], BF16)
        UB_QB = sbt("UB_QB", [128, NT * D], BF16)
        UB_KE = sbt("UB_KE", [128, NT * D], BF16)
        XT = sbt("XT", [128, D], F32)
        ATM = sbt("ATM", [128, D], BF16)
        TMP2 = sbt("TMP2", [128, D], BF16)
        TMP3 = sbt("TMP3", [128, D], BF16)
        WSTR = sbt("WSTR", [128, 2 * D], BF16)
        BALL = sbt("BALL", [128, D], BF16)
        WREP = sbt("WREP", [128, D], BF16)
        WBREP = sbt("WBREP", [128, D], BF16)
        BREP = sbt("BREP", [128, D], BF16)
        NWREP = sbt("NWREP", [128, D], BF16)
        WMIX = sbt("WMIX", [128, 512], F32)
        GMASK = sbt("GMASK", [128, 256], F32)
        IDENT = sbt("IDENT", [128, 128], BF16)
        IDF32 = sbt("IDF32", [8, 8], F32)
        EMAT = sbt("EMAT", [8, 128], BF16)
        S32 = sbt("S32", [128, H * DH], F32)
        SB16 = sbt("SB16", [128, H * DH], BF16)
        VNS = sbt("VNS", [128, H * DH], BF16)
        GSB = sbt("GSB", [128, H * 256], BF16)
        KHAT = sbt("KHAT", [128, 512], F32)
        GWB = sbt("GWB", [128, 512], F32)
        KHB = sbt("KHB", [128, 512], BF16)
        GWBB = sbt("GWBB", [128, 512], BF16)
        WMIXB = sbt("WMIXB", [128, 512], BF16)
        STB = sbt("STB", [128, 12], BF16)
        GSDB = sbt("GSDB", [128, 256], BF16)
        VNPK = sbt("VNPK", [128, 512], BF16)
        OPK = sbt("OPK", [128, 512], BF16)
        UQPK = sbt("UQPK", [128, 1024], BF16)
        STA = sbt("STA", [128, 40], F32)
        EPSA = sbt("EPSA", [128, 2], F32)
        ETAN = sbt("ETAN", [128, NT * H], F32)
        BETAS = sbt("BETAS", [16, BLK], BF16)
        G1F = sbt("G1F", [128, BLK], BF16)
        KS = sbt("KS", [128, NT * H], F32)
        RSK = sbt("RSK", [128, NT * H], F32)
        OGS = sbt("OGS", [128, H], F32)
        MUB = sbt("MUB", [128, NT], F32)
        RB = sbt("RB", [128, NT], F32)
        LGT = sbt("LGT", [8, BLK], F32)
        LTM = sbt("LTM", [128, NT * 8], F32)
        CW1 = sbt("CW1", [128, NT * 8], F32)
        CWB = sbt("CWB", [128, NT * 8], BF16)
        CWF = sbt("CWF", [8, BLK], BF16)
        CWX = sbt("CWX", [128, BLK], BF16)
        ZCW = sbt("ZCW", [128, BLK], BF16)

        KQ = pst("KQ", [128, 2048], F32)
        PP = pst("PP", [128, 1024], F32)
        TP = pst("TP", [128, 1024], BF16)
        GD = pst("GD", [128, 512], F32)

        LD = Sem(st, nc, "LD")
        WL = Sem(st, nc, "WL")
        WH = Sem(st, nc, "WH")
        WP = Sem(st, nc, "WP")
        MM = Sem(st, nc, "MM")
        EV = Sem(st, nc, "EV")
        SE = Sem(st, nc, "SE")
        GP = Sem(st, nc, "GP")
        SH = Sem(st, nc, "SH")
        SO = Sem(st, nc, "SO")
        ALLSEMS = [LD, WL, WH, MM, EV, SE, GP, SH, SO]

        blk = st.enter_context(nc.Block())

        def vfence(v, instr):
            EV.inc(instr, 1)
            v.wait_ge(EV.h, EV.n)

        def barrier():
            snap = [(s, s.n) for s in ALLSEMS]
            for sec in (blk.tensor, blk.vector, blk.scalar, blk.gpsimd, blk.sync):
                def _f(e, snap=snap):
                    for s, n in snap:
                        if n > 0:
                            e.wait_ge(s.h, n)
                sec(_f)

        IOT1 = KHAT
        IOT2 = GWB

        def _setup_g(g):
            g.iota(IOT1[:, 0:256], pattern=[[0, 256]], base=0,
                   channel_multiplier=1, allow_small_or_imprecise_dtypes=True)
            g.iota(IOT2[:, 0:256], pattern=[[16, 8], [0, 32]], base=0,
                   channel_multiplier=0, allow_small_or_imprecise_dtypes=True)
            g.iota(IOT2[:, 256:512], pattern=[[16, 8], [0, 2], [1, 16]], base=0,
                   channel_multiplier=0, allow_small_or_imprecise_dtypes=True)
            g.iota(IOT1[:, 256:384], pattern=[[1, 128]], base=0,
                   channel_multiplier=0, allow_small_or_imprecise_dtypes=True)
            gi = g.iota(KHAT[0:8, 384:512], pattern=[[1, 8], [0, 16]], base=0,
                        channel_multiplier=0, allow_small_or_imprecise_dtypes=True)
            GP.inc(gi, 1)
            for dst, srcr in [(WREP, lnw.rearrange("h d -> (h d)")),
                              (BREP, lnb.rearrange("h d -> (h d)")),
                              (NWREP, nrmw)]:
                dd = g.dma_start(out=dst[:, :], in_=srcr.unsqueeze(0).broadcast_to([128, D]))
                WL.inc(dd, 16)
            for dst, srcw in [(wbg_b, wbg), (wq_b, wq), (wk_b, wk), (wv_b, wv),
                              (wg2_b, wg2), (wo_b, wo), (gwT_b, gwT),
                              (a_all_b, a_all), (wfT_b, wfT)]:
                dd = g.dma_start(out=dst[:, :], in_=srcw[:, :])
                WP.inc(dd, 16)
            dd = g.dma_start(out=BALL[:, :], in_=b_all[:, :])
            WL.inc(dd, 16)
            for g4 in range(4):
                for h4 in range(4):
                    src = lnw[4 * g4 + h4:4 * g4 + h4 + 1, :].broadcast_to([32, DH])
                    dw = g.dma_start(out=WMIX[32 * h4:32 * h4 + 32, 128 * g4:128 * g4 + 128],
                                     in_=src)
                    WL.inc(dw, 16)
        blk.gpsimd(_setup_g)
        gp_iota = GP.n
        wl_setup = WL.n

        def _setup_v(v):
            v.wait_ge(GP.h, gp_iota)
            v.tensor_tensor(IDENT[:, :], IOT1[:, 0:128], IOT1[:, 256:384], ALU.is_equal)
            v.tensor_tensor(IDF32[:, :], IOT1[0:8, 0:8], IOT1[0:8, 256:264], ALU.is_equal)
            v.tensor_tensor(GMASK[:, :].rearrange("p (c x t) -> p c x t", c=8, x=2)[:, :, 0, :],
                            IOT1[:, 0:256].rearrange("p (c x t) -> p c x t", c=8, x=2)[:, :, 0, :],
                            IOT2[:, 0:256].rearrange("p (c x t) -> p c x t", c=8, x=2)[:, :, 0, :],
                            ALU.is_lt)
            v.tensor_tensor(GMASK[:, :].rearrange("p (c x t) -> p c x t", c=8, x=2)[:, :, 1, :],
                            IOT1[:, 0:256].rearrange("p (c x t) -> p c x t", c=8, x=2)[:, :, 1, :],
                            IOT2[:, 256:512].rearrange("p (c x t) -> p c x t", c=8, x=2)[:, :, 1, :],
                            ALU.is_le)
            v.tensor_tensor(EMAT[:, :], IOT1[0:8, 0:128], KHAT[0:8, 384:512], ALU.is_equal)
            v.memset(EPSA[:, 0:1], EPS_LN)
            v.memset(EPSA[:, 1:2], EPS_IN)
            v.memset(S32[:, :], 0.0)
            v.memset(SB16[:, :], 0.0)
            iv = v.memset(VNS[:, :], 0.0)
            v.wait_ge(WL.h, wl_setup)
            for h4 in range(4):
                iv = v.tensor_tensor(WMIX[32 * h4:32 * h4 + 16, :],
                                     WMIX[32 * h4:32 * h4 + 16, :],
                                     WMIX[32 * h4:32 * h4 + 16, :], ALU.mult)
            v.tensor_tensor(WBREP[:, :], WREP[:, :], BREP[:, :], ALU.mult)
            iv = v.tensor_copy(WMIXB[:, :], WMIX[:, :])
            EV.inc(iv, 1)
        blk.vector(_setup_v)
        barrier()

        def stream_wcol(w_ap, n, slot, ncols=128):
            def _g(g):
                src = w_ap[:, n * ncols:(n + 1) * ncols].rearrange("(k p) c -> p k c", p=128)
                dw = g.dma_start(out=WSTR[:, slot * D:slot * D + KT * ncols].rearrange(
                    "p (k c) -> p k c", k=KT), in_=src)
                WL.inc(dw, 16)
            blk.gpsimd(_g)
            return WL.n

        def tp_transposes(srcs, dst_fn, wait=()):
            nsl = len(srcs)

            def _t(t):
                for s, n in wait:
                    t.wait_ge(s.h, n)
                t.wait_ge(EV.h, EV.n)
                for c, sap in enumerate(srcs):
                    it = t.transpose(TP[:, c * 128:(c + 1) * 128], sap, IDENT[:, :])
                MM.inc(it, 1)
            blk.tensor(_t)
            mmn = MM.n

            def _v(v):
                v.wait_ge(MM.h, mmn)
                iv = dst_fn(v, TP[:, 0:nsl * 128])
                EV.inc(iv, 1)
            blk.vector(_v)

        def do_block(b):
            t0 = b * BLK
            nhalf = BLK // 512

            def _pl(s):
                s.preload_act_table(AF.Sqrt)
            blk.scalar(_pl)
            for j in range(NT):
                so_before = SO.n

                def _y(sy, j=j, so_before=so_before):
                    sy.wait_ge(SO.h, so_before)
                    sy.wait_ge(EV.h, EV.n)
                    dx = sy.dma_start(out=XT[:, :], in_=xin[t0 + j * 128:t0 + (j + 1) * 128, :])
                    LD.inc(dx, 16)
                blk.sync(_y)
                ld_x = LD.n

                def _s(s, j=j, ld_x=ld_x):
                    s.wait_ge(LD.h, ld_x)
                    s.wait_ge(EV.h, EV.n)
                    s.activation(PP[:, :], XT[:, 0:1024], AF.Square, accum_out=STA[:, 16:17])
                    i2 = s.activation(KQ[:, 0:1024], XT[:, 1024:2048], AF.Square, accum_out=STA[:, 17:18])
                    SE.inc(i2, 1)
                blk.scalar(_s)
                se_sq = SE.n

                def _v(v, j=j, ld_x=ld_x, se_sq=se_sq):
                    v.wait_ge(LD.h, ld_x)
                    v.tensor_reduce(STA[:, 18:19], XT[:, :], AX.X, ALU.add)
                    v.wait_ge(SE.h, se_sq)
                    v.tensor_add(STA[:, 19:20], STA[:, 16:17], STA[:, 17:18])
                    i0 = v.tensor_tensor(STA[:, 20:21], STA[:, 18:19], STA[:, 18:19], ALU.mult)
                    vfence(v, i0)
                    iv = v.scalar_tensor_tensor(STA[:, 21:22], STA[:, 20:21], -1.0 / D,
                                                STA[:, 19:20], ALU.mult, ALU.add)
                    EV.inc(iv, 1)
                blk.vector(_v)
                ev_vx = EV.n

                def _s2(s, j=j, ev_vx=ev_vx):
                    s.wait_ge(EV.h, ev_vx)
                    i3 = s.activation(STA[:, 22:23], STA[:, 21:22], AF.Sqrt,
                                      bias=EPSA[:, 0:1], scale=1.0 / D)
                    SE.inc(i3, 1)
                blk.scalar(_s2)
                se_std = SE.n

                def _v2(v, j=j, se_std=se_std):
                    v.wait_ge(SE.h, se_std)
                    v.reciprocal(RB[:, j:j + 1], STA[:, 22:23])
                    i0 = v.tensor_scalar_mul(MUB[:, j:j + 1], STA[:, 18:19], 1.0 / D)
                    vfence(v, i0)
                    iv = v.tensor_scalar(ATM[:, :], XT[:, :], MUB[:, j:j + 1],
                                         RB[:, j:j + 1], ALU.subtract, ALU.mult)
                    EV.inc(iv, 1)
                blk.vector(_v2)
                ev_xh = EV.n

                for half in range(2):
                    def mkdst(j=j, half=half):
                        def dst(v, tp):
                            return v.tensor_copy(
                                BIGA[:, :].rearrange("p (k t) -> p k t", k=KT)[
                                    :, half * 8:half * 8 + 8, j * 128:(j + 1) * 128],
                                tp.rearrange("p (c t) -> p c t", c=8))
                        return dst
                    tp_transposes([ATM[:, (half * 8 + c) * 128:(half * 8 + c + 1) * 128]
                                   for c in range(8)], mkdst(), wait=[(EV, ev_xh)])
            barrier()
            if dbg and b == 0:
                def _yd(sy):
                    dd = sy.dma_start(out=dbg_t["d_xh1f"][:, :], in_=BIGA[:, :])
                    WL.inc(dd, 16)
                blk.gpsimd(_yd)

            def _g(g):
                g.wait_ge(WP.h, 16)
                dw = g.dma_start(out=WSTR[:, 0:KT * 144], in_=wbg_b[:, :])
                WH.inc(dw, 16)
            blk.sync(_g)
            wl_wbg = WH.n

            def _t(t, wl_wbg=wl_wbg):
                t.wait_ge(WH.h, wl_wbg)
                for k in range(KT):
                    t.matmul(PP[:, 0:512], WSTR[:, k * 144:k * 144 + 128],
                             BIGA[:, k * BLK:(k + 1) * BLK],
                             start=(k == 0), stop=(k == KT - 1))
                for k in range(KT):
                    it = t.matmul(KQ[0:16, 0:512], WSTR[:, k * 144 + 128:k * 144 + 144],
                                  BIGA[:, k * BLK:(k + 1) * BLK],
                                  start=(k == 0), stop=(k == KT - 1))
                MM.inc(it, 1)
            blk.tensor(_t)
            mm_wbg = MM.n
            big_proj_hist_seed = mm_wbg

            def _s(s, mm_wbg=mm_wbg):
                s.wait_ge(MM.h, mm_wbg)
                s.preload_act_table(AF.Sigmoid)
                s.copy(G1F[:, :], PP[:, 0:BLK])
                i2 = s.activation(BETAS[:, :], KQ[0:16, 0:BLK], AF.Sigmoid)
                SE.inc(i2, 1)
            blk.scalar(_s)
            se_beta = SE.n

            def _t2(t, se_beta=se_beta):
                t.wait_ge(SE.h, se_beta)
                t.wait_ge(EV.h, EV.n)
                for j in range(NT):
                    it = t.transpose(TP[:, j * 16:j * 16 + 16],
                                     BETAS[0:16, j * 128:(j + 1) * 128], IDENT[0:16, 0:16])
                MM.inc(it, 1)
            blk.tensor(_t2)
            mm_eta = MM.n

            def _v(v, mm_eta=mm_eta):
                v.wait_ge(MM.h, mm_eta)
                iv = v.tensor_scalar_mul(ETAN[:, :], TP[:, 0:NT * 16], -2.0)
                EV.inc(iv, 1)
            blk.vector(_v)
            ev_eta = EV.n
            barrier()
            if dbg and b == 0:
                def _yd(sy):
                    dd = sy.dma_start(out=dbg_t["d_etan"][:, :], in_=ETAN[:, :])
                    WL.inc(dd, 16)
                blk.gpsimd(_yd)

            def pslot(s):
                return PP[:, (s % 2) * 512:(s % 2) * 512 + 512] if s < 2 else \
                       KQ[:, (s - 2) * 512:(s - 1) * 512]

            big_proj_state = {"hist": {0: big_proj_hist_seed, 1: big_proj_hist_seed},
                              "se_hist": {2: se_beta, 3: se_beta, 4: se_beta, 5: se_beta},
                              "ev_hist": {}}

            def big_proj(w_ap, src_buf, evac, wp_gate=0):
                for n in range(KT):
                    slot = n % 2
                    mm_prev2 = big_proj_state["hist"].get(slot, 0)

                    def _g(g, n=n, slot=slot, mm_prev2=mm_prev2, wp_gate=wp_gate):
                        if wp_gate:
                            g.wait_ge(WP.h, wp_gate)
                        g.wait_ge(MM.h, mm_prev2)
                        dw = g.dma_start(out=WSTR[:, slot * D:(slot + 1) * D],
                                         in_=w_ap[:, n * D:(n + 1) * D])
                        WH.inc(dw, 16)
                    blk.sync(_g)
                    wl_n = WH.n

                    def _t(t, n=n, slot=slot, wl_n=wl_n,
                           se_prev=big_proj_state["se_hist"].get(n % 6, 0),
                           ev_prev=big_proj_state["ev_hist"].get(n % 6, 0)):
                        t.wait_ge(WH.h, wl_n)
                        if se_prev:
                            t.wait_ge(SE.h, se_prev)
                        if ev_prev:
                            t.wait_ge(EV.h, ev_prev)
                        for k in range(KT):
                            it = t.matmul(pslot(n % 6)[:, k * 0:512] if False else pslot(n % 6),
                                          WSTR[:, slot * D + k * 128:slot * D + (k + 1) * 128],
                                          src_buf[:, k * BLK:(k + 1) * BLK],
                                          start=(k == 0), stop=(k == KT - 1))
                        MM.inc(it, 1)
                    blk.tensor(_t)
                    big_proj_state["hist"][slot] = MM.n
                    evac(n, MM.n, n % 6)
                    big_proj_state["se_hist"][n % 6] = SE.n
                    big_proj_state["ev_hist"][n % 6] = EV.n

            def evac_q(n, mm_n, off=0):
                def _s(s, n=n, mm_n=mm_n, off=off):
                    s.wait_ge(MM.h, mm_n)
                    dstq = BIGB[:, n * 2 * BLK:(n + 1) * 2 * BLK].rearrange(
                        "p (c x t) -> p c x t", x=2, t=16)[:, :, 1, :]
                    s.activation(dstq, pslot(off), AF.Silu)
                    i1 = s.activation(ATM[:, 0:BLK], pslot(off), AF.Silu)
                    SE.inc(i1, 1)
                blk.scalar(_s)
                se_n = SE.n

                def _t(t, n=n, se_n=se_n):
                    t.wait_ge(SE.h, se_n)
                    t.wait_ge(EV.h, EV.n)
                    for j in range(NT):
                        it = t.transpose(TP[:, j * 128:(j + 1) * 128],
                                         ATM[:, j * 128:(j + 1) * 128], IDENT[:, :])
                    MM.inc(it, 1)
                blk.tensor(_t)
                mm_t = MM.n

                def _v(v, n=n, mm_t=mm_t):
                    v.wait_ge(MM.h, mm_t)
                    dst = UB_QB[:, :].rearrange("p (j c) -> p j c", j=NT)[:, :, n * 128:(n + 1) * 128]
                    iv = v.tensor_tensor(
                        dst, TP[:, 0:NT * 128].rearrange("p (j c) -> p j c", j=NT),
                        BREP[:, :].rearrange("p (h d) -> p h d", h=H)[:, n:n + 1, :].broadcast_to([128, NT, 128]),
                        ALU.add)
                    EV.inc(iv, 1)
                blk.vector(_v)
            def _plq(s):
                s.preload_act_table(AF.Silu)
            blk.scalar(_plq)
            big_proj(wq_b, BIGA, evac_q, wp_gate=32)

            def evac_k(n, mm_n, off=0):
                def _s(s, n=n, mm_n=mm_n, off=off):
                    s.wait_ge(MM.h, mm_n)
                    i1 = s.copy(ATM[:, 0:BLK], pslot(off))
                    SE.inc(i1, 1)
                blk.scalar(_s)
                se_n = SE.n

                def _t(t, n=n, se_n=se_n):
                    t.wait_ge(SE.h, se_n)
                    t.wait_ge(EV.h, EV.n)
                    for j in range(NT):
                        it = t.transpose(TP[:, j * 128:(j + 1) * 128],
                                         ATM[:, j * 128:(j + 1) * 128], IDENT[:, :])
                    MM.inc(it, 1)
                blk.tensor(_t)
                mm_t = MM.n

                def _sq(s, n=n, mm_t=mm_t):
                    s.wait_ge(MM.h, mm_t)
                    i1 = s.activation(GWB[:, :], TP[:, 0:NT * 128], AF.Square)
                    SE.inc(i1, 1)
                blk.scalar(_sq)
                se_sq = SE.n

                def _v(v, n=n, se_sq=se_sq):
                    v.wait_ge(SE.h, se_sq)
                    iv = v.tensor_reduce(STA[:, 0:NT],
                                         GWB[:, 0:NT * 128].rearrange("p (j d) -> p j d", j=NT),
                                         AX.X, ALU.add)
                    EV.inc(iv, 1)
                blk.vector(_v)
                ev_ks = EV.n

                def _s2(s, n=n, ev_ks=ev_ks):
                    s.wait_ge(EV.h, ev_ks)
                    i1 = s.activation(STA[:, 4:4 + NT], STA[:, 0:NT], AF.Sqrt, bias=EPSA[:, 1:2])
                    SE.inc(i1, 1)
                blk.scalar(_s2)
                se_rk = SE.n

                def _v2(v, n=n, se_rk=se_rk):
                    v.wait_ge(SE.h, se_rk)
                    v.wait_ge(EV.h, ev_eta)
                    i0 = v.reciprocal(STA[:, 8:8 + NT], STA[:, 4:4 + NT])
                    vfence(v, i0)
                    dst_u = UB_U[:, :].rearrange("p (j c) -> p j c", j=NT)[:, :, n * 128:(n + 1) * 128]
                    i0 = v.tensor_tensor(dst_u,
                                         TP[:, 0:NT * 128].rearrange("p (j d) -> p j d", j=NT),
                                         STA[:, 8:8 + NT].unsqueeze(2).broadcast_to([128, NT, DH]),
                                         ALU.mult)
                    vfence(v, i0)
                    dst_ke = UB_KE[:, :].rearrange("p (j c) -> p j c", j=NT)[:, :, n * 128:(n + 1) * 128]
                    iv = v.tensor_tensor(dst_ke, dst_u,
                                         ETAN[:, :].rearrange("p (j h) -> p j h", j=NT)[
                                             :, :, n:n + 1].broadcast_to([128, NT, DH]),
                                         ALU.mult)
                    EV.inc(iv, 1)
                blk.vector(_v2)
                ev_kn = EV.n

                def _t2(t, n=n, ev_kn=ev_kn):
                    t.wait_ge(EV.h, ev_kn)
                    for j in range(NT):
                        it = t.transpose(TP[:, 512 + j * 128:512 + (j + 1) * 128],
                                         UB_U[:, j * D + n * 128:j * D + (n + 1) * 128],
                                         IDENT[:, :])
                    MM.inc(it, 1)
                blk.tensor(_t2)
                mm_t2 = MM.n

                def _v3(v, n=n, mm_t2=mm_t2):
                    v.wait_ge(MM.h, mm_t2)
                    dstk = BIGB[:, n * 2 * BLK:(n + 1) * 2 * BLK].rearrange(
                        "p (j c x t) -> p j c x t", j=NT, c=8, x=2)[:, :, :, 0, :]
                    v.tensor_copy(dstk,
                                  TP[:, 512:1024].rearrange("p (j c t) -> p j c t", j=NT, c=8))
                    iv = v.tensor_copy(KSC[:, n * BLK:(n + 1) * BLK], TP[:, 512:1024])
                    EV.inc(iv, 1)
                blk.vector(_v3)
            def _plk(s):
                s.preload_act_table(AF.Sqrt)
            blk.scalar(_plk)
            big_proj(wk_b, BIGA, evac_k, wp_gate=48)
            barrier()

            def evac_v(n, mm_n, off=0):
                def _s(s, n=n, mm_n=mm_n, off=off):
                    s.wait_ge(MM.h, mm_n)
                    s.wait_ge(EV.h, EV.n)
                    i1 = s.copy(ATM[:, 0:BLK], pslot(off))
                    SE.inc(i1, 1)
                blk.scalar(_s)
                se_v = SE.n

                def _t(t, n=n, se_v=se_v):
                    t.wait_ge(SE.h, se_v)
                    t.wait_ge(EV.h, EV.n)
                    for j in range(NT):
                        it = t.transpose(TP[:, j * 128:(j + 1) * 128],
                                         ATM[:, j * 128:(j + 1) * 128], IDENT[:, :])
                    MM.inc(it, 1)
                blk.tensor(_t)
                mm_vt = MM.n

                def _v(v, n=n, mm_vt=mm_vt):
                    v.wait_ge(MM.h, mm_vt)
                    for j in range(NT):
                        i0 = v.tensor_tensor(TMP2[:, j * 128:(j + 1) * 128],
                                             UB_U[:, j * D + n * 128:j * D + (n + 1) * 128],
                                             TP[:, j * 128:(j + 1) * 128], ALU.subtract)
                    vfence(v, i0)
                    for j in range(NT):
                        i0 = v.tensor_tensor(TMP3[:, j * 128:(j + 1) * 128],
                                             TMP2[:, j * 128:(j + 1) * 128],
                                             WREP[:, n * 128:(n + 1) * 128], ALU.mult)
                    vfence(v, i0)
                    iv = None
                    for j in range(NT):
                        iv = v.tensor_tensor(UB_U[:, j * D + n * 128:j * D + (n + 1) * 128],
                                             TMP3[:, j * 128:(j + 1) * 128],
                                             WBREP[:, n * 128:(n + 1) * 128], ALU.add)
                    EV.inc(iv, 1)
                blk.vector(_v)
            big_proj(wv_b, BIGA, evac_v, wp_gate=64)
            barrier()

            ev_pass2 = {}
            sh_vn = {}
            sh_uq = {}
            sh_o = {}
            mm_diag = {}
            state = {"ev_supd": EV.n, "ev_gram": EV.n}
            for l in range(NT):
                sc = b * NT + l

                for hp in range(8):
                    prev_ev = state.get("ev_gpair", EV.n)

                    def _tg(t, l=l, hp=hp, prev_ev=prev_ev):
                        t.wait_ge(EV.h, prev_ev)
                        if hp == 0:
                            t.wait_ge(EV.h, state["ev_supd"])
                        for q in range(2):
                            h = hp * 2 + q
                            it = t.matmul(GD[:, q * 256:(q + 1) * 256],
                                          KSC[:, h * BLK + l * 128:h * BLK + (l + 1) * 128],
                                          BIGB[:, h * 2 * BLK + l * 256:h * 2 * BLK + (l + 1) * 256],
                                          start=True, stop=True)
                        MM.inc(it, 1)
                    blk.tensor(_tg)
                    mm_g = MM.n

                    def _v(v, l=l, hp=hp, mm_g=mm_g):
                        v.wait_ge(MM.h, mm_g)
                        iv = None
                        for q in range(2):
                            h = hp * 2 + q
                            iv = v.scalar_tensor_tensor(GSB[:, h * 256:(h + 1) * 256],
                                                        GD[:, q * 256:(q + 1) * 256],
                                                        ETAN[:, l * H + h:l * H + h + 1],
                                                        GMASK[:, :], ALU.mult, ALU.mult)
                        EV.inc(iv, 1)
                    blk.vector(_v)
                    state["ev_gpair"] = EV.n
                state["ev_gram"] = EV.n
                if dbg and sc == 0:
                    def _gd3(g, evn=EV.n):
                        g.wait_ge(EV.h, evn)
                        dd = g.dma_start(out=dbg_t["d_gsb"][:, :], in_=GSB[:, :])
                        WL.inc(dd, 16)
                    blk.gpsimd(_gd3)

                def _vz(v, sc=sc, mmn=MM.n):
                    v.wait_ge(MM.h, mmn)
                    iv = v.memset(VNS[:, :], 0.0)
                    EV.inc(iv, 1)
                blk.vector(_vz)
                state["ev_vnz"] = EV.n

                def _y0(sy, l=l, sc=sc):
                    i0 = sc * 8
                    slot = i0 % 2
                    sy.wait_ge(EV.h, ev_pass2.get(i0 - 2, 0))
                    for h4 in range(4):
                        s_u = UB_U[0:16, :].rearrange("p (j g x d) -> p j g x d",
                                                      j=NT, g=4, x=4)[:, l, :, h4, :]
                        d_u = UQPK[32 * h4:32 * h4 + 16,
                                   slot * 512:(slot + 1) * 512].rearrange("p (g d) -> p g d", g=4)
                        dd = sy.dma_start(out=d_u, in_=s_u)
                        WL.inc(dd, 16)
                        s_q = UB_QB[0:16, :].rearrange("p (j g x d) -> p j g x d",
                                                       j=NT, g=4, x=4)[:, l, :, h4, :]
                        d_q = UQPK[32 * h4 + 16:32 * h4 + 32,
                                   slot * 512:(slot + 1) * 512].rearrange("p (g d) -> p g d", g=4)
                        dd = sy.dma_start(out=d_q, in_=s_q)
                        WL.inc(dd, 16)
                blk.gpsimd(_y0)
                sh_uq[sc * 8] = WL.n

                for il in range(8):
                    i = sc * 8 + il
                    slot = i % 4
                    csl = KQ[:, slot * 512:(slot + 1) * 512]

                    def _yg(sy, i=i, il=il, evg=state["ev_gram"]):
                        sy.wait_ge(EV.h, evg)
                        if i >= 2:
                            sy.wait_ge(MM.h, mm_diag.get(i - 2, 0))
                        gsl = (i % 2) * 128
                        for h4 in range(4):
                            src_ap = GSB[16 * il:16 * il + 16, :].rearrange(
                                "p (g c) -> p g c", g=4)[:, :, h4 * 256 + il * 32:h4 * 256 + (il + 1) * 32]
                            dst_ap = GSDB[32 * h4:32 * h4 + 16, gsl:gsl + 128].rearrange(
                                "p (g c) -> p g c", g=4)
                            dd = sy.dma_start(out=dst_ap, in_=src_ap)
                            WL.inc(dd, 16)
                    blk.gpsimd(_yg)
                    sh_gd = WL.n

                    def _t(t, l=l, il=il, i=i, slot=slot):
                        t.wait_ge(EV.h, ev_pass2.get(i - 4, 0))
                        t.wait_ge(EV.h, state["ev_supd"])
                        for h in range(H):
                            lhs = BIGB[:, h * 2 * BLK + l * 256 + il * 32:h * 2 * BLK + l * 256 + (il + 1) * 32]
                            it = t.matmul(KQ[32 * (h % 4):32 * (h % 4) + 32,
                                             slot * 512 + (h // 4) * 128:slot * 512 + (h // 4) * 128 + 128],
                                          lhs, SB16[:, h * DH:(h + 1) * DH],
                                          start=(h < 4), stop=False,
                                          skip_group_check=True,
                                          tile_position=(0, 32 * (h % 4)))
                        MM.inc(it, 1)
                    blk.tensor(_t)

                    def _tp(t, l=l, il=il, i=i, slot=slot,
                            evg=state["ev_gram"], evz=state["ev_vnz"]):
                        if il > 0:
                            t.wait_ge(EV.h, evg)
                            t.wait_ge(EV.h, evz)
                            t.wait_ge(SH.h, sh_vn[i - 1])
                        kpre = 32 * ((il + 1) // 2)
                        if il > 0:
                            for h in range(H):
                                glhs = GSB[0:kpre, h * 256 + il * 32:h * 256 + (il + 1) * 32]
                                it = t.matmul(KQ[32 * (h % 4):32 * (h % 4) + 32,
                                                 slot * 512 + (h // 4) * 128:slot * 512 + (h // 4) * 128 + 128],
                                              glhs, VNS[0:kpre, h * DH:(h + 1) * DH],
                                              start=False, stop=False,
                                              skip_group_check=True,
                                              tile_position=(0, 32 * (h % 4)))
                        else:
                            it = t.nop()
                        MM.inc(it, 1)
                    blk.tensor(_tp)
                    mm_kq = MM.n

                    def _s(s, i=i, il=il, csl=csl, mm_kq=mm_kq, evg=state["ev_gram"]):
                        s.wait_ge(MM.h, mm_kq)
                        if il == 0:
                            s.wait_ge(EV.h, evg)
                        i1 = s.activation(GD[:, :], csl, AF.Square)
                        SE.inc(i1, 1)
                    blk.scalar(_s)
                    se_sq1 = SE.n

                    def _v(v, i=i, csl=csl, se_sq1=se_sq1):
                        v.wait_ge(SE.h, se_sq1)
                        v.tensor_reduce(STA[:, 4:8], csl.rearrange("p (g d) -> p g d", g=4),
                                        AX.X, ALU.add)
                        v.tensor_reduce(STA[:, 0:4], GD[:, :].rearrange("p (g d) -> p g d", g=4),
                                        AX.X, ALU.add)
                        i0 = v.tensor_tensor(STA[:, 8:12], STA[:, 4:8], STA[:, 4:8], ALU.mult)
                        vfence(v, i0)
                        iv = v.scalar_tensor_tensor(STA[:, 12:16], STA[:, 8:12], -1.0 / DH,
                                                    STA[:, 0:4], ALU.mult, ALU.add)
                        EV.inc(iv, 1)
                    blk.vector(_v)
                    ev_vx = EV.n

                    def _s2(s, i=i, ev_vx=ev_vx):
                        s.wait_ge(EV.h, ev_vx)
                        i3 = s.activation(STA[:, 16:20], STA[:, 12:16], AF.Sqrt,
                                          bias=EPSA[:, 1:2], scale=1.0 / DH)
                        SE.inc(i3, 1)
                    blk.scalar(_s2)
                    se_std = SE.n

                    def _v2(v, i=i, csl=csl, se_std=se_std, sh_uq_n=sh_uq[i]):
                        v.wait_ge(SE.h, se_std)
                        v.wait_ge(WL.h, sh_uq_n)
                        H2 = [dict() for _ in range(2)]
                        for h in range(2):
                            c0 = 256 * h
                            H2[h]["csl"] = csl[:, c0:c0 + 256]
                            H2[h]["K"] = KHB[:, c0:c0 + 256]
                            H2[h]["Ks"] = KHB[:, c0:c0 + 256].rearrange("p (g d) -> p g d", g=2)
                            H2[h]["G"] = GWBB[:, c0:c0 + 256]
                            H2[h]["Gs"] = GWBB[:, c0:c0 + 256].rearrange("p (g d) -> p g d", g=2)
                            H2[h]["O"] = VNPK[:, c0:c0 + 256]
                            H2[h]["W"] = WMIXB[:, c0:c0 + 256]
                            H2[h]["U"] = UQPK[:, (i % 2) * 512 + c0:(i % 2) * 512 + c0 + 256]
                        v.reciprocal(STA[:, 20:24], STA[:, 16:20])
                        v.tensor_scalar_mul(STA[:, 24:28], STA[:, 4:8], 1.0 / DH)
                        v.tensor_copy(STB[:, 0:4], STA[:, 20:24])
                        for h in range(2):
                            v.tensor_tensor(H2[h]["Ks"],
                                            H2[h]["csl"].rearrange("p (g d) -> p g d", g=2),
                                            STA[:, 24 + 2 * h:26 + 2 * h].unsqueeze(2).broadcast_to([128, 2, DH]),
                                            ALU.subtract)
                        for h in range(2):
                            v.tensor_tensor(H2[h]["Ks"], H2[h]["Ks"],
                                            STB[:, 2 * h:2 * h + 2].unsqueeze(2).broadcast_to([128, 2, DH]),
                                            ALU.mult)
                        for h in range(2):
                            v.tensor_tensor(H2[h]["G"], H2[h]["K"], H2[h]["W"], ALU.mult)
                        for h in range(2):
                            v.tensor_tensor(H2[h]["G"], H2[h]["G"], H2[h]["U"], ALU.add)
                        for h in range(2):
                            v.tensor_tensor(H2[h]["O"], H2[h]["G"], H2[h]["K"], ALU.mult)
                        v.tensor_reduce(STA[:, 28:32], GWBB[:, :].rearrange("p (g d) -> p g d", g=4),
                                        AX.X, ALU.add)
                        v.tensor_reduce(STA[:, 32:36], VNPK[:, :].rearrange("p (g d) -> p g d", g=4),
                                        AX.X, ALU.add)
                        i0 = v.scalar_tensor_tensor(STA[:, 8:12], STA[:, 28:32], -1.0 / DH,
                                                    STA[:, 20:24], ALU.mult, ALU.mult)
                        i1 = v.scalar_tensor_tensor(STA[:, 36:40], STA[:, 32:36], -1.0 / DH,
                                                    STA[:, 20:24], ALU.mult, ALU.mult)
                        vfence(v, i1)
                        v.tensor_copy(STB[:, 4:8], STA[:, 36:40])
                        v.tensor_copy(STB[:, 8:12], STA[:, 8:12])
                        for h in range(2):
                            v.tensor_tensor(H2[h]["Ks"], H2[h]["Ks"],
                                            STB[:, 4 + 2 * h:6 + 2 * h].unsqueeze(2).broadcast_to([128, 2, DH]),
                                            ALU.mult)
                        for h in range(2):
                            v.tensor_tensor(H2[h]["Gs"], H2[h]["Gs"],
                                            STB[:, 2 * h:2 * h + 2].unsqueeze(2).broadcast_to([128, 2, DH]),
                                            ALU.mult)
                        for h in range(2):
                            v.tensor_tensor(H2[h]["Ks"], H2[h]["Ks"],
                                            STB[:, 8 + 2 * h:10 + 2 * h].unsqueeze(2).broadcast_to([128, 2, DH]),
                                            ALU.add)
                        iv = None
                        for h in range(2):
                            iv = v.tensor_tensor(VNPK[:, 256 * h:256 * h + 256],
                                                 H2[h]["G"], H2[h]["K"], ALU.add)
                        EV.inc(iv, 1)
                    blk.vector(_v2)
                    ev_k = EV.n

                    def _y(sy, i=i, il=il, l=l, ev_k=ev_k):
                        sy.wait_ge(EV.h, ev_k)
                        for h4 in range(4):
                            src = VNPK[32 * h4:32 * h4 + 16, :].rearrange("p (g d) -> p g d", g=4)
                            dst = VNS[16 * il:16 * il + 16, :].rearrange(
                                "p (g x d) -> p g x d", g=4, x=4)[:, :, h4, :]
                            dd = sy.dma_start(out=dst, in_=src)
                            SH.inc(dd, 16)
                    blk.sync(_y)
                    sh_vn[i] = SH.n

                    if il < 7:
                        def _y2(sy, i=i, il=il, l=l):
                            sy.wait_ge(EV.h, ev_pass2.get(i - 1, 0))
                            slot2 = (i + 1) % 2
                            for h4 in range(4):
                                s_u = UB_U[16 * (il + 1):16 * (il + 2), :].rearrange(
                                    "p (j g x d) -> p j g x d", j=NT, g=4, x=4)[:, l, :, h4, :]
                                d_u = UQPK[32 * h4:32 * h4 + 16,
                                           slot2 * 512:(slot2 + 1) * 512].rearrange("p (g d) -> p g d", g=4)
                                dd = sy.dma_start(out=d_u, in_=s_u)
                                WL.inc(dd, 16)
                                s_q = UB_QB[16 * (il + 1):16 * (il + 2), :].rearrange(
                                    "p (j g x d) -> p j g x d", j=NT, g=4, x=4)[:, l, :, h4, :]
                                d_q = UQPK[32 * h4 + 16:32 * h4 + 32,
                                           slot2 * 512:(slot2 + 1) * 512].rearrange("p (g d) -> p g d", g=4)
                                dd = sy.dma_start(out=d_q, in_=s_q)
                                WL.inc(dd, 16)
                        blk.gpsimd(_y2)
                        sh_uq[i + 1] = WL.n

                    def _t2(t, i=i, il=il, slot=slot, ev_k=ev_k, sh_gd=sh_gd):
                        t.wait_ge(EV.h, ev_k)
                        t.wait_ge(WL.h, sh_gd)
                        gsl = (i % 2) * 128
                        for h in range(H):
                            glhs = GSDB[32 * (h % 4):32 * (h % 4) + 16,
                                        gsl + (h // 4) * 32:gsl + (h // 4 + 1) * 32]
                            it = t.matmul(KQ[32 * (h % 4):32 * (h % 4) + 32,
                                             slot * 512 + (h // 4) * 128:slot * 512 + (h // 4) * 128 + 128],
                                          glhs,
                                          VNPK[32 * (h % 4):32 * (h % 4) + 16,
                                               (h // 4) * 128:(h // 4 + 1) * 128],
                                          start=False, stop=False,
                                          skip_group_check=True,
                                          tile_position=(32 * (h % 4), 32 * (h % 4)))
                        MM.inc(it, 1)
                    blk.tensor(_t2)
                    mm_dg = MM.n
                    mm_diag[i] = mm_dg

                    if dbg and b == 0 and i in (0, 1):
                        def _vd(v, i=i, csl=csl, mm_dg=mm_dg):
                            v.wait_ge(MM.h, mm_dg)
                            iv = v.tensor_copy(KHAT[:, :], csl)
                            EV.inc(iv, 1)
                        blk.vector(_vd)
                        ev_dsc = EV.n

                        def _gd(g, i=i, ev_dsc=ev_dsc):
                            g.wait_ge(EV.h, ev_dsc)
                            dd = g.dma_start(out=dbg_t["d_kq%d" % i][:, :], in_=KHAT[:, :])
                            WL.inc(dd, 16)
                            dd = g.dma_start(out=dbg_t["d_vnpk%d" % i][:, :], in_=VNPK[:, :])
                            WL.inc(dd, 16)
                        blk.gpsimd(_gd)

                    def _s5(s, i=i, csl=csl, mm_dg=mm_dg):
                        s.wait_ge(MM.h, mm_dg)
                        i6 = s.activation(GD[:, :], csl, AF.Square)
                        SE.inc(i6, 1)
                    blk.scalar(_s5)
                    se_sq2 = SE.n

                    def _v5(v, i=i, csl=csl, se_sq2=se_sq2):
                        v.wait_ge(SE.h, se_sq2)
                        v.tensor_reduce(STA[:, 4:8], csl.rearrange("p (g d) -> p g d", g=4),
                                        AX.X, ALU.add)
                        v.tensor_reduce(STA[:, 0:4], GD[:, :].rearrange("p (g d) -> p g d", g=4),
                                        AX.X, ALU.add)
                        i0 = v.tensor_tensor(STA[:, 8:12], STA[:, 4:8], STA[:, 4:8], ALU.mult)
                        vfence(v, i0)
                        iv = v.scalar_tensor_tensor(STA[:, 12:16], STA[:, 8:12], -1.0 / DH,
                                                    STA[:, 0:4], ALU.mult, ALU.add)
                        EV.inc(iv, 1)
                    blk.vector(_v5)
                    ev_vx2 = EV.n

                    def _s6(s, i=i, ev_vx2=ev_vx2):
                        s.wait_ge(EV.h, ev_vx2)
                        i7 = s.activation(STA[:, 16:20], STA[:, 12:16], AF.Sqrt,
                                          bias=EPSA[:, 1:2], scale=1.0 / DH)
                        SE.inc(i7, 1)
                    blk.scalar(_s6)
                    se_std2 = SE.n

                    def _v6(v, i=i, csl=csl, se_std2=se_std2,
                            sh_o_n=sh_o.get(i - 1, 0)):
                        v.wait_ge(SE.h, se_std2)
                        v.reciprocal(STA[:, 20:24], STA[:, 16:20])
                        v.tensor_scalar_mul(STA[:, 24:28], STA[:, 4:8], 1.0 / DH)
                        v.tensor_copy(STB[:, 0:4], STA[:, 20:24])
                        for h in range(2):
                            c0 = 256 * h
                            v.tensor_tensor(KHB[:, c0:c0 + 256].rearrange("p (g d) -> p g d", g=2),
                                            csl[:, c0:c0 + 256].rearrange("p (g d) -> p g d", g=2)
                                            if False else
                                            csl.rearrange("p (x gd) -> p x gd", x=2)[:, h, :].rearrange("p (g d) -> p g d", g=2),
                                            STA[:, 24 + 2 * h:26 + 2 * h].unsqueeze(2).broadcast_to([128, 2, DH]),
                                            ALU.subtract)
                        for h in range(2):
                            c0 = 256 * h
                            v.tensor_tensor(KHB[:, c0:c0 + 256].rearrange("p (g d) -> p g d", g=2),
                                            KHB[:, c0:c0 + 256].rearrange("p (g d) -> p g d", g=2),
                                            STB[:, 2 * h:2 * h + 2].unsqueeze(2).broadcast_to([128, 2, DH]),
                                            ALU.mult)
                        for h in range(2):
                            c0 = 256 * h
                            v.tensor_tensor(GWBB[:, c0:c0 + 256], KHB[:, c0:c0 + 256],
                                            WMIXB[:, c0:c0 + 256], ALU.mult)
                        if sh_o_n:
                            v.wait_ge(SH.h, sh_o_n)
                        iv = None
                        for h in range(2):
                            c0 = 256 * h
                            iv = v.tensor_tensor(OPK[:, c0:c0 + 256], GWBB[:, c0:c0 + 256],
                                                 UQPK[:, (i % 2) * 512 + c0:(i % 2) * 512 + c0 + 256],
                                                 ALU.add)
                        EV.inc(iv, 1)
                    blk.vector(_v6)
                    ev_pass2[i] = EV.n

                    def _y3(sy, i=i, il=il, l=l):
                        sy.wait_ge(EV.h, ev_pass2[i])
                        for h4 in range(4):
                            src = OPK[32 * h4 + 16:32 * h4 + 32, :].rearrange("p (g d) -> p g d", g=4)
                            dst = BIGA[16 * il:16 * il + 16, :].rearrange(
                                "p (j g x d) -> p j g x d", j=NT, g=4, x=4)[:, l, :, h4, :]
                            dd = sy.dma_start(out=dst, in_=src)
                            SH.inc(dd, 16)
                    blk.sync(_y3)
                    sh_o[i] = SH.n

                if dbg and sc == 0:
                    def _gd4(g, shn=sh_vn[sc * 8 + 7]):
                        g.wait_ge(SH.h, shn)
                        dd = g.dma_start(out=dbg_t["d_vns"][:, :], in_=VNS[:, :])
                        WL.inc(dd, 16)
                    blk.gpsimd(_gd4)

                for w in range(4):
                    prev = state.get("ev_supd_w", EV.n)

                    def _t(t, l=l, w=w, prev=prev, sc=sc, evp=ev_pass2[sc * 8 + 7]):
                        t.wait_ge(EV.h, prev)
                        t.wait_ge(EV.h, evp)
                        t.wait_ge(SH.h, sh_vn[sc * 8 + 7])
                        t.wait_ge(SE.h, SE.n)
                        for hh in range(4):
                            h = w * 4 + hh
                            it = t.matmul(GD[:, hh * 128:(hh + 1) * 128],
                                          UB_KE[:, l * D + h * DH:l * D + (h + 1) * DH],
                                          VNS[:, h * DH:(h + 1) * DH],
                                          start=True, stop=True)
                        MM.inc(it, 1)
                    blk.tensor(_t)
                    mm_ds = MM.n

                    def _v(v, w=w, mm_ds=mm_ds):
                        v.wait_ge(MM.h, mm_ds)
                        i0 = v.tensor_add(S32[:, 512 * w:512 * (w + 1)],
                                          S32[:, 512 * w:512 * (w + 1)], GD[:, :])
                        vfence(v, i0)
                        iv = v.tensor_copy(SB16[:, 512 * w:512 * (w + 1)],
                                           S32[:, 512 * w:512 * (w + 1)])
                        EV.inc(iv, 1)
                    blk.vector(_v)
                    state["ev_supd_w"] = EV.n
                state["ev_supd"] = EV.n

            barrier()
            if dbg and b == 0:
                def _yd(sy):
                    dd = sy.dma_start(out=dbg_t["d_ot"][:, :], in_=BIGA[:, 0:NT * D])
                    WL.inc(dd, 16)
                    dd = sy.dma_start(out=dbg_t["d_s32"][:, :], in_=S32[:, :])
                    WL.inc(dd, 16)
                blk.gpsimd(_yd)

            def _plg(s):
                s.preload_act_table(AF.Sigmoid)
            blk.scalar(_plg)

            def _g2(g):
                g.wait_ge(WP.h, 80)
                g.wait_ge(MM.h, MM.n)
                dw = g.dma_start(out=WSTR[:, 0:D], in_=wg2_b[:, :])
                WH.inc(dw, 16)
            blk.sync(_g2)
            wl_g2 = WH.n
            for j in range(NT):
                for half in range(2):
                    def _t(t, j=j, half=half, wl_g2=wl_g2):
                        t.wait_ge(WH.h, wl_g2)
                        t.wait_ge(SE.h, SE.n)
                        for nn in range(2):
                            it = t.matmul(PP[:, nn * 512:(nn + 1) * 512],
                                          G1F[:, j * 128:(j + 1) * 128],
                                          WSTR[:, half * 1024 + nn * 512:half * 1024 + (nn + 1) * 512],
                                          start=True, stop=True)
                        MM.inc(it, 1)
                    blk.tensor(_t)
                    mm_g2 = MM.n

                    def _s(s, j=j, half=half, mm_g2=mm_g2):
                        s.wait_ge(MM.h, mm_g2)
                        i1 = s.activation(SGB[:, j * D + half * 1024:j * D + (half + 1) * 1024],
                                          PP[:, :], AF.Sigmoid)
                        SE.inc(i1, 1)
                    blk.scalar(_s)
            barrier()
            if dbg and b == 0:
                def _yd(sy):
                    for nm, buf in [("d_kqf", BIGB), ("d_u", UB_U), ("d_qb", UB_QB),
                                    ("d_ke", UB_KE), ("d_sg", SGB)]:
                        dd = sy.dma_start(out=dbg_t[nm][:, :], in_=buf[:, :])
                        WL.inc(dd, 16)
                blk.gpsimd(_yd)

            def _plo(s):
                s.preload_act_table(AF.Sqrt)
            blk.scalar(_plo)
            for j in range(NT):
                def _v(v, j=j):
                    v.wait_ge(EV.h, EV.n)
                    iv = v.tensor_tensor(TMP2[:, :], BIGA[:, j * D:(j + 1) * D],
                                         SGB[:, j * D:(j + 1) * D], ALU.mult)
                    EV.inc(iv, 1)
                blk.vector(_v)
                ev_og = EV.n

                def _s(s, j=j, ev_og=ev_og):
                    s.wait_ge(EV.h, ev_og)
                    i1 = s.activation(TMP3[:, :], TMP2[:, :], AF.Square)
                    SE.inc(i1, 1)
                blk.scalar(_s)
                se_og2 = SE.n

                def _v2(v, j=j, se_og2=se_og2):
                    v.wait_ge(SE.h, se_og2)
                    iv = v.tensor_reduce(OGS[:, :], TMP3[:, :].rearrange("p (h d) -> p h d", h=H),
                                         AX.X, ALU.add)
                    EV.inc(iv, 1)
                blk.vector(_v2)
                ev_ogs = EV.n

                def _s2(s, j=j, ev_ogs=ev_ogs):
                    s.wait_ge(EV.h, ev_ogs)
                    i1 = s.activation(STA[:, 0:H], OGS[:, :], AF.Sqrt, bias=EPSA[:, 1:2], scale=1.0 / DH)
                    SE.inc(i1, 1)
                blk.scalar(_s2)
                se_ogr = SE.n

                def _v3(v, j=j, se_ogr=se_ogr):
                    v.wait_ge(SE.h, se_ogr)
                    i0 = v.reciprocal(STA[:, H:2 * H], STA[:, 0:H])
                    vfence(v, i0)
                    i0 = v.tensor_tensor(ATM[:, :].rearrange("p (h d) -> p h d", h=H),
                                         TMP2[:, :].rearrange("p (h d) -> p h d", h=H),
                                         STA[:, H:2 * H].unsqueeze(2).broadcast_to([128, H, DH]),
                                         ALU.mult)
                    vfence(v, i0)
                    iv = v.tensor_tensor(ATM[:, :], ATM[:, :], NWREP[:, :], ALU.mult)
                    EV.inc(iv, 1)
                blk.vector(_v3)
                ev_o2 = EV.n

                for half in range(2):
                    def mkdst4(j=j, half=half):
                        def dst(v, tp):
                            return v.tensor_copy(
                                BIGB[:, 0:KT * BLK].rearrange("p (k t) -> p k t", k=KT)[
                                    :, half * 8:half * 8 + 8, j * 128:(j + 1) * 128],
                                tp.rearrange("p (c t) -> p c t", c=8))
                        return dst
                    tp_transposes([ATM[:, (half * 8 + c) * 128:(half * 8 + c + 1) * 128]
                                   for c in range(8)], mkdst4(), wait=[(EV, ev_o2)])
            barrier()
            if dbg and b == 0:
                def _yd(sy):
                    dd = sy.dma_start(out=dbg_t["d_o2f"][:, :], in_=BIGB[:, 0:KT * BLK])
                    WL.inc(dd, 16)
                blk.gpsimd(_yd)

            def evac_wo(n, mm_n, off=0):
                def _s(s, n=n, mm_n=mm_n, off=off):
                    s.wait_ge(MM.h, mm_n)
                    i1 = s.copy(BIGA[:, n * BLK:(n + 1) * BLK], pslot(off))
                    SE.inc(i1, 1)
                blk.scalar(_s)
            big_proj(wo_b, BIGB, evac_wo, wp_gate=96)
            barrier()
            if dbg and b == 0:
                def _yd(sy):
                    dd = sy.dma_start(out=dbg_t["d_attnf"][:, :], in_=BIGA[:, :])
                    WL.inc(dd, 16)
                blk.gpsimd(_yd)

            for j in range(NT):
                for half in range(2):
                    def mkdst5(j=j, half=half):
                        def dst(v, tp):
                            return v.tensor_copy(
                                ATM[:, half * 1024:(half + 1) * 1024].rearrange(
                                    "p (c t) -> p c t", c=8),
                                tp.rearrange("p (c t) -> p c t", c=8))
                        return dst
                    tp_transposes([BIGA[:, (half * 8 + c) * BLK + j * 128:
                                        (half * 8 + c) * BLK + (j + 1) * 128]
                                   for c in range(8)], mkdst5())
                ev_at = EV.n

                def _y(sy, j=j):
                    sy.wait_ge(EV.h, EV.n)
                    dx = sy.dma_start(out=XT[:, :], in_=xin[t0 + j * 128:t0 + (j + 1) * 128, :])
                    LD.inc(dx, 16)
                blk.sync(_y)
                ld_x2 = LD.n

                def _s(s, j=j, ev_at=ev_at):
                    s.wait_ge(EV.h, ev_at)
                    s.wait_ge(EV.h, EV.n)
                    s.activation(PP[:, :], ATM[:, 0:1024], AF.Square, accum_out=STA[:, 16:17])
                    i2 = s.activation(KQ[:, 0:1024], ATM[:, 1024:2048], AF.Square, accum_out=STA[:, 17:18])
                    SE.inc(i2, 1)
                blk.scalar(_s)
                se_sq = SE.n

                def _v(v, j=j, ev_at=ev_at, se_sq=se_sq):
                    v.wait_ge(EV.h, ev_at)
                    v.tensor_reduce(STA[:, 18:19], ATM[:, :], AX.X, ALU.add)
                    v.wait_ge(SE.h, se_sq)
                    v.tensor_add(STA[:, 19:20], STA[:, 16:17], STA[:, 17:18])
                    i0 = v.tensor_tensor(STA[:, 20:21], STA[:, 18:19], STA[:, 18:19], ALU.mult)
                    vfence(v, i0)
                    iv = v.scalar_tensor_tensor(STA[:, 21:22], STA[:, 20:21], -1.0 / D,
                                                STA[:, 19:20], ALU.mult, ALU.add)
                    EV.inc(iv, 1)
                blk.vector(_v)
                ev_vx = EV.n

                def _s2(s, j=j, ev_vx=ev_vx):
                    s.wait_ge(EV.h, ev_vx)
                    i3 = s.activation(STA[:, 22:23], STA[:, 21:22], AF.Sqrt,
                                      bias=EPSA[:, 0:1], scale=1.0 / D)
                    SE.inc(i3, 1)
                blk.scalar(_s2)
                se_std = SE.n

                def _v2(v, j=j, se_std=se_std, ld_x2=ld_x2):
                    v.wait_ge(SE.h, se_std)
                    v.wait_ge(LD.h, ld_x2)
                    v.reciprocal(STA[:, 23:24], STA[:, 22:23])
                    i0 = v.tensor_scalar_mul(STA[:, 24:25], STA[:, 18:19], 1.0 / D)
                    vfence(v, i0)
                    i0 = v.tensor_scalar(TMP2[:, :], ATM[:, :], STA[:, 24:25], STA[:, 23:24],
                                         ALU.subtract, ALU.mult)
                    vfence(v, i0)
                    iv = v.tensor_tensor(XT[:, :], XT[:, :], TMP2[:, :], ALU.add)
                    EV.inc(iv, 1)
                blk.vector(_v2)
                ev_x2 = EV.n

                def _y2(sy, j=j, ev_x2=ev_x2):
                    sy.wait_ge(EV.h, ev_x2)
                    dd = sy.dma_start(out=out[t0 + j * 128:t0 + (j + 1) * 128, :], in_=XT[:, :])
                    SO.inc(dd, 16)
                blk.gpsimd(_y2)

                def _s3(s, j=j, ev_x2=ev_x2):
                    s.wait_ge(EV.h, ev_x2)
                    s.activation(PP[:, :], XT[:, 0:1024], AF.Square, accum_out=STA[:, 16:17])
                    i2 = s.activation(KQ[:, 0:1024], XT[:, 1024:2048], AF.Square, accum_out=STA[:, 17:18])
                    SE.inc(i2, 1)
                blk.scalar(_s3)
                se_sq3 = SE.n

                def _v3(v, j=j, ev_x2=ev_x2, se_sq3=se_sq3):
                    v.wait_ge(EV.h, ev_x2)
                    v.tensor_reduce(STA[:, 18:19], XT[:, :], AX.X, ALU.add)
                    v.wait_ge(SE.h, se_sq3)
                    v.tensor_add(STA[:, 19:20], STA[:, 16:17], STA[:, 17:18])
                    i0 = v.tensor_tensor(STA[:, 20:21], STA[:, 18:19], STA[:, 18:19], ALU.mult)
                    vfence(v, i0)
                    iv = v.scalar_tensor_tensor(STA[:, 21:22], STA[:, 20:21], -1.0 / D,
                                                STA[:, 19:20], ALU.mult, ALU.add)
                    EV.inc(iv, 1)
                blk.vector(_v3)
                ev_vx3 = EV.n

                def _s4(s, j=j, ev_vx3=ev_vx3):
                    s.wait_ge(EV.h, ev_vx3)
                    i3 = s.activation(STA[:, 22:23], STA[:, 21:22], AF.Sqrt,
                                      bias=EPSA[:, 0:1], scale=1.0 / D)
                    SE.inc(i3, 1)
                blk.scalar(_s4)
                se_std3 = SE.n

                def _v4(v, j=j, se_std3=se_std3):
                    v.wait_ge(SE.h, se_std3)
                    v.reciprocal(STA[:, 23:24], STA[:, 22:23])
                    i0 = v.tensor_scalar_mul(STA[:, 24:25], STA[:, 18:19], 1.0 / D)
                    vfence(v, i0)
                    iv = v.tensor_scalar(ATM[:, :], XT[:, :], STA[:, 24:25], STA[:, 23:24],
                                         ALU.subtract, ALU.mult)
                    EV.inc(iv, 1)
                blk.vector(_v4)
                ev_x3h = EV.n

                for half in range(2):
                    def mkdst6(j=j, half=half):
                        def dst(v, tp):
                            return v.tensor_copy(
                                BIGB[:, 0:KT * BLK].rearrange("p (k t) -> p k t", k=KT)[
                                    :, half * 8:half * 8 + 8, j * 128:(j + 1) * 128],
                                tp.rearrange("p (c t) -> p c t", c=8))
                        return dst
                    tp_transposes([ATM[:, (half * 8 + c) * 128:(half * 8 + c + 1) * 128]
                                   for c in range(8)], mkdst6(), wait=[(EV, ev_x3h)])
            barrier()
            if dbg and b == 0:
                def _yd(sy):
                    dd = sy.dma_start(out=dbg_t["d_x3f"][:, :], in_=BIGB[:, 0:KT * BLK])
                    WL.inc(dd, 16)
                blk.gpsimd(_yd)

            def _g(g):
                g.wait_ge(WP.h, 112)
                g.wait_ge(MM.h, MM.n)
                dw = g.dma_start(out=WSTR[:, 0:KT * 8], in_=gwT_b[:, :])
                WH.inc(dw, 16)
            blk.sync(_g)
            wl_gw = WH.n

            def _t(t, wl_gw=wl_gw):
                t.wait_ge(WH.h, wl_gw)
                for k in range(KT):
                    it = t.matmul(GD[0:8, 0:512], WSTR[:, k * 8:(k + 1) * 8],
                                  BIGB[:, k * BLK:(k + 1) * BLK],
                                  start=(k == 0), stop=(k == KT - 1))
                MM.inc(it, 1)
            blk.tensor(_t)
            mm_lg = MM.n

            def _v(v, mm_lg=mm_lg):
                v.wait_ge(MM.h, mm_lg)
                iv = v.tensor_copy(LGT[0:8, :], GD[0:8, 0:BLK])
                EV.inc(iv, 1)
            blk.vector(_v)
            ev_lg = EV.n

            def _t3(t, ev_lg=ev_lg):
                t.wait_ge(EV.h, ev_lg)
                for j in range(NT):
                    it = t.transpose(GD[:, j * 8:j * 8 + 8].bitcast(F32) if False else GD[:, j * 8:j * 8 + 8],
                                     LGT[0:8, j * 128:(j + 1) * 128], IDF32[0:8, 0:8])
                MM.inc(it, 1)
            blk.tensor(_t3)
            mm_lt = MM.n

            def _v2(v, mm_lt=mm_lt):
                v.wait_ge(MM.h, mm_lt)
                iv = v.tensor_copy(LTM[:, :], GD[:, 0:NT * 8])
                EV.inc(iv, 1)
            blk.vector(_v2)
            ev_ltm = EV.n

            def _v4(v, ev_ltm=ev_ltm):
                v.wait_ge(EV.h, ev_ltm)
                for j in range(NT):
                    i0 = v.tensor_reduce(STA[:, j:j + 1], LTM[:, j * 8:(j + 1) * 8],
                                         AX.X, ALU.max)
                vfence(v, i0)
                for j in range(NT):
                    i0 = v.tensor_scalar(CW1[:, j * 8:(j + 1) * 8], LTM[:, j * 8:(j + 1) * 8],
                                         STA[:, j:j + 1], None, ALU.is_ge, ALU.bypass)
                vfence(v, i0)
                for j in range(NT):
                    i0 = v.scalar_tensor_tensor(CW1[:, j * 8:(j + 1) * 8],
                                                CW1[:, j * 8:(j + 1) * 8], -1e30,
                                                LTM[:, j * 8:(j + 1) * 8], ALU.mult, ALU.add)
                vfence(v, i0)
                for j in range(NT):
                    i0 = v.tensor_reduce(STA[:, 4 + j:5 + j], CW1[:, j * 8:(j + 1) * 8],
                                         AX.X, ALU.max)
                vfence(v, i0)
                for j in range(NT):
                    v.tensor_scalar(CW1[:, j * 8:(j + 1) * 8], LTM[:, j * 8:(j + 1) * 8],
                                    STA[:, 4 + j:5 + j], None, ALU.is_ge, ALU.bypass)
                for j in range(NT):
                    iv = v.tensor_scalar(LTM[:, j * 8:(j + 1) * 8], LTM[:, j * 8:(j + 1) * 8],
                                         STA[:, j:j + 1], None, ALU.subtract, ALU.bypass)
                    EV.inc(iv, 1) if j == NT - 1 else None
            blk.vector(_v4)
            ev_cwpre = EV.n

            def _s5(s, ev_cwpre=ev_cwpre):
                s.wait_ge(EV.h, ev_cwpre)
                s.preload_act_table(AF.Exp)
                i1 = s.activation(LTM[:, :], LTM[:, :], AF.Exp)
                SE.inc(i1, 1)
            blk.scalar(_s5)
            se_exp = SE.n

            def _v5(v, se_exp=se_exp):
                v.wait_ge(SE.h, se_exp)
                for j in range(NT):
                    i0 = v.tensor_tensor(LTM[:, j * 8:(j + 1) * 8], LTM[:, j * 8:(j + 1) * 8],
                                         CW1[:, j * 8:(j + 1) * 8], ALU.mult)
                vfence(v, i0)
                for j in range(NT):
                    i0 = v.tensor_reduce(STA[:, 8 + j:9 + j], LTM[:, j * 8:(j + 1) * 8],
                                         AX.X, ALU.add)
                vfence(v, i0)
                for j in range(NT):
                    i0 = v.reciprocal(STA[:, 12 + j:13 + j], STA[:, 8 + j:9 + j])
                vfence(v, i0)
                iv = None
                for j in range(NT):
                    iv = v.tensor_scalar(CWB[:, j * 8:(j + 1) * 8], LTM[:, j * 8:(j + 1) * 8],
                                         STA[:, 12 + j:13 + j], 16.0, ALU.mult, ALU.mult)
                EV.inc(iv, 1)
            blk.vector(_v5)
            ev_cwb = EV.n

            def _t4(t, ev_cwb=ev_cwb):
                t.wait_ge(EV.h, ev_cwb)
                t.wait_ge(EV.h, EV.n)
                for j in range(NT):
                    it = t.transpose(TP[0:8, j * 128:j * 128 + 128],
                                     CWB[:, j * 8:(j + 1) * 8], IDENT[:, :])
                MM.inc(it, 1)
            blk.tensor(_t4)
            mm_cwt = MM.n

            def _v6(v, mm_cwt=mm_cwt):
                v.wait_ge(MM.h, mm_cwt)
                iv = v.tensor_copy(CWF[0:8, :], TP[0:8, 0:NT * 128])
                EV.inc(iv, 1)
            blk.vector(_v6)
            ev_cwf = EV.n

            def _t5(t, ev_cwf=ev_cwf):
                t.wait_ge(EV.h, ev_cwf)
                t.wait_ge(EV.h, EV.n)
                it = t.matmul(PP[:, 0:512], EMAT[0:8, :], CWF[0:8, 0:512], start=True, stop=True)
                MM.inc(it, 1)
            blk.tensor(_t5)
            mm_cwx = MM.n

            def _v7(v, mm_cwx=mm_cwx):
                v.wait_ge(MM.h, mm_cwx)
                iv = v.tensor_copy(CWX[:, :], PP[:, 0:BLK])
                EV.inc(iv, 1)
            blk.vector(_v7)
            ev_cwx = EV.n
            if dbg and b == 0:
                def _yd(sy, ev_cwf=ev_cwf):
                    sy.wait_ge(EV.h, ev_cwf)
                    dd = sy.dma_start(out=dbg_t["d_cwf"][:, :], in_=CWF[0:8, :])
                    WL.inc(dd, 16)
                blk.gpsimd(_yd)

            def _g2(g):
                g.wait_ge(WP.h, 128)
                g.wait_ge(MM.h, MM.n)
                dw = g.dma_start(out=WSTR[:, D:D + KT * 128], in_=a_all_b[:, :])
                WH.inc(dw, 16)
            blk.sync(_g2)
            wl_a = WH.n

            def _t6(t, wl_a=wl_a, ev_cwx=ev_cwx):
                t.wait_ge(WH.h, wl_a)
                t.wait_ge(EV.h, ev_cwx)
                for k in range(KT):
                    it = t.matmul(PP[:, 0:512], WSTR[:, D + k * 128:D + (k + 1) * 128],
                                  BIGB[:, k * BLK:(k + 1) * BLK],
                                  start=(k == 0), stop=(k == KT - 1))
                MM.inc(it, 1)
            blk.tensor(_t6)
            mm_z = MM.n

            def _v8(v, mm_z=mm_z, ev_cwx=ev_cwx):
                v.wait_ge(MM.h, mm_z)
                v.wait_ge(EV.h, ev_cwx)
                iv = v.tensor_tensor(ZCW[:, :], PP[:, 0:BLK], CWX[:, :], ALU.mult)
                EV.inc(iv, 1)
            blk.vector(_v8)
            ev_zcw = EV.n
            do_block._ffn_hist = {0: mm_z, 1: mm_z}
            if dbg and b == 0:
                def _yd(sy, ev_zcw=ev_zcw):
                    sy.wait_ge(EV.h, ev_zcw)
                    dd = sy.dma_start(out=dbg_t["d_zcw"][:, :], in_=ZCW[:, :])
                    WL.inc(dd, 16)
                blk.gpsimd(_yd)

            ffn_pp_hist = {}
            def evac_ffn(n, mm_n, off=0):
                def _s(s, n=n, mm_n=mm_n, off=off):
                    s.wait_ge(MM.h, mm_n)
                    i1 = s.copy(BIGA[:, n * BLK:(n + 1) * BLK], pslot(off))
                    SE.inc(i1, 1)
                blk.scalar(_s)

            for n in range(KT):
                slot = n % 2
                mm_prev2 = getattr(do_block, "_ffn_hist", {}).get(slot, 0)

                def _g3(g, n=n, slot=slot, mm_prev2=mm_prev2):
                    g.wait_ge(WP.h, 144)
                    g.wait_ge(MM.h, mm_prev2)
                    dw = g.dma_start(out=WSTR[:, slot * D:(slot + 1) * D],
                                     in_=wfT_b[:, n * D:(n + 1) * D])
                    WH.inc(dw, 16)
                blk.sync(_g3)
                wl_n = WH.n

                def _t7(t, n=n, slot=slot, wl_n=wl_n, ev_zcw=ev_zcw,
                        se_prev=ffn_pp_hist.get(n % 6, 0)):
                    t.wait_ge(WH.h, wl_n)
                    t.wait_ge(EV.h, ev_zcw)
                    if se_prev:
                        t.wait_ge(SE.h, se_prev)
                    for k in range(KT):
                        t.matmul(pslot(n % 6),
                                 WSTR[:, slot * D + k * 128:slot * D + (k + 1) * 128],
                                 BIGB[:, k * BLK:(k + 1) * BLK],
                                 start=(k == 0), stop=False)
                    it = t.matmul(pslot(n % 6),
                                  BALL[:, n * 128:(n + 1) * 128],
                                  ZCW[:, :], start=False, stop=True)
                    MM.inc(it, 1)
                blk.tensor(_t7)
                hist = getattr(do_block, "_ffn_hist", {})
                hist[slot] = MM.n
                do_block._ffn_hist = hist
                evac_ffn(n, MM.n, n % 6)
                ffn_pp_hist[n % 6] = SE.n
            barrier()

            def _pl4(s):
                s.preload_act_table(AF.Sqrt)
            blk.scalar(_pl4)
            for j in range(NT):
                for half in range(2):
                    def mkdst7(j=j, half=half):
                        def dst(v, tp):
                            return v.tensor_copy(
                                ATM[:, half * 1024:(half + 1) * 1024].rearrange(
                                    "p (c t) -> p c t", c=8),
                                tp.rearrange("p (c t) -> p c t", c=8))
                        return dst
                    tp_transposes([BIGA[:, (half * 8 + c) * BLK + j * 128:
                                        (half * 8 + c) * BLK + (j + 1) * 128]
                                   for c in range(8)], mkdst7())
                ev_ft = EV.n

                so_before = SO.n

                def _y(sy, j=j, so_before=so_before):
                    sy.wait_ge(SO.h, so_before)
                    sy.wait_ge(EV.h, EV.n)
                    dx = sy.dma_start(out=XT[:, :], in_=out[t0 + j * 128:t0 + (j + 1) * 128, :])
                    LD.inc(dx, 16)
                blk.sync(_y)
                ld_x2r = LD.n

                def _s(s, j=j, ev_ft=ev_ft):
                    s.wait_ge(EV.h, ev_ft)
                    s.wait_ge(EV.h, EV.n)
                    s.activation(PP[:, :], ATM[:, 0:1024], AF.Square, accum_out=STA[:, 16:17])
                    i2 = s.activation(KQ[:, 0:1024], ATM[:, 1024:2048], AF.Square, accum_out=STA[:, 17:18])
                    SE.inc(i2, 1)
                blk.scalar(_s)
                se_sq4 = SE.n

                def _v(v, j=j, ev_ft=ev_ft, se_sq4=se_sq4):
                    v.wait_ge(EV.h, ev_ft)
                    v.tensor_reduce(STA[:, 18:19], ATM[:, :], AX.X, ALU.add)
                    v.wait_ge(SE.h, se_sq4)
                    v.tensor_add(STA[:, 19:20], STA[:, 16:17], STA[:, 17:18])
                    i0 = v.tensor_tensor(STA[:, 20:21], STA[:, 18:19], STA[:, 18:19], ALU.mult)
                    vfence(v, i0)
                    iv = v.scalar_tensor_tensor(STA[:, 21:22], STA[:, 20:21], -1.0 / D,
                                                STA[:, 19:20], ALU.mult, ALU.add)
                    EV.inc(iv, 1)
                blk.vector(_v)
                ev_vx4 = EV.n

                def _s2(s, j=j, ev_vx4=ev_vx4):
                    s.wait_ge(EV.h, ev_vx4)
                    i3 = s.activation(STA[:, 22:23], STA[:, 21:22], AF.Sqrt,
                                      bias=EPSA[:, 0:1], scale=1.0 / D)
                    SE.inc(i3, 1)
                blk.scalar(_s2)
                se_std4 = SE.n

                def _v2(v, j=j, se_std4=se_std4, ld_x2r=ld_x2r):
                    v.wait_ge(SE.h, se_std4)
                    v.wait_ge(LD.h, ld_x2r)
                    v.reciprocal(STA[:, 23:24], STA[:, 22:23])
                    i0 = v.tensor_scalar_mul(STA[:, 24:25], STA[:, 18:19], 1.0 / D)
                    vfence(v, i0)
                    i0 = v.tensor_scalar(TMP2[:, :], ATM[:, :], STA[:, 24:25], STA[:, 23:24],
                                         ALU.subtract, ALU.mult)
                    vfence(v, i0)
                    iv = v.tensor_tensor(XT[:, :], XT[:, :], TMP2[:, :], ALU.add)
                    EV.inc(iv, 1)
                blk.vector(_v2)
                ev_x3 = EV.n

                def _y2(sy, j=j, ev_x3=ev_x3):
                    sy.wait_ge(EV.h, ev_x3)
                    dd = sy.dma_start(out=out[t0 + j * 128:t0 + (j + 1) * 128, :], in_=XT[:, :])
                    SO.inc(dd, 16)
                blk.gpsimd(_y2)
            barrier()

        for b in range(NBLK):
            do_block(b)

        def _fin(sy):
            sy.wait_ge(SO.h, SO.n)
        blk.sync(_fin)

    return nc


_NC_CACHE = {}


def kernel(**inputs):
    T = 2048
    x = np.ascontiguousarray(np.asarray(inputs["L0_attn_x"], dtype=np.float32))
    def colblock(w, ncols=128):
        Din, Dout = w.shape
        nk, nn = Din // 128, Dout // ncols
        return np.ascontiguousarray(
            w.reshape(nk, 128, nn, ncols).transpose(1, 2, 0, 3).reshape(128, nn * nk * ncols))

    base = {
        "wq": colblock(np.asarray(inputs["L0_attn_Wq"], dtype=np.float32)),
        "wk": colblock(np.asarray(inputs["L0_attn_Wk"], dtype=np.float32)),
        "wv": colblock(np.asarray(inputs["L0_attn_Wv"], dtype=np.float32)),
        "wo": colblock(np.asarray(inputs["L0_attn_Wo"], dtype=np.float32)),
        "wbg": colblock(np.concatenate(
            [np.asarray(inputs["L0_attn_Wg1"], dtype=np.float32),
             np.asarray(inputs["L0_attn_Wbeta"], dtype=np.float32)], axis=1), ncols=144),
        "wg2": np.ascontiguousarray(np.asarray(inputs["L0_attn_Wg2"], dtype=np.float32)),
        "lnw": np.ascontiguousarray(np.asarray(inputs["L0_attn_ln_w"], dtype=np.float32)),
        "lnb": np.ascontiguousarray(np.asarray(inputs["L0_attn_ln_b"], dtype=np.float32)),
        "nrmw": np.ascontiguousarray(np.asarray(inputs["L0_attn_norm_w"], dtype=np.float32)),
        "wfT": colblock(np.ascontiguousarray(np.asarray(inputs["L0_ffn_weight"], dtype=np.float32).T)),
        "gwT": colblock(np.ascontiguousarray(np.asarray(inputs["L0_ffn_gate_w"], dtype=np.float32).T), ncols=8),
        "a_all": colblock(np.ascontiguousarray(np.asarray(
            inputs["L0_ffn_A_w"], dtype=np.float32).transpose(2, 0, 1).reshape(D, 128))),
        "b_all": np.ascontiguousarray(np.asarray(
            inputs["L0_ffn_B_w"], dtype=np.float32).transpose(0, 2, 1).reshape(128, D)),
    }
    if "nc" not in _NC_CACHE:
        nc = bass.Bass("TRN2", target_bir_lowering=False, debug=False)
        nc.detect_race_conditions = False
        build(nc, T)
        nc.finalize()
        _NC_CACHE["nc"] = nc
    nc = _NC_CACHE["nc"]
    in_maps = []
    for c in range(8):
        m = dict(base)
        m["x"] = np.ascontiguousarray(x[c])
        in_maps.append(m)
    res = run_bass_kernel_spmd(nc, in_maps, core_ids=list(range(8)))
    return np.stack([r["out"] for r in res.results], axis=0)
